# version 03
# baseline (speedup 1.0000x reference)
import numpy as np
import ml_dtypes
from contextlib import ExitStack

import concourse.bass as bass
import concourse.mybir as mybir
from concourse.bass_utils import run_bass_kernel_spmd
from concourse.bass_core_rust import IndirectOffsetOnAxis

F32 = mybir.dt.float32
BF16 = mybir.dt.bfloat16
I32 = mybir.dt.int32
U32 = mybir.dt.uint32
AF = mybir.ActivationFunctionType
ALU = mybir.AluOpType
AX = mybir.AxisListType
BF16NP = ml_dtypes.bfloat16

B = 8
T = 4096
D = 1024
NH0 = 16
HD0 = 64
BLK0 = 512
NB0 = T // BLK0
E0 = 8
CAP0 = 5120
CLOC = 1536
NSLOT = E0 * CLOC
E1 = 16
TOPK = 4
NH1 = 16
ROPE = 32
LAT = 128
NT = T // 128
BIGIDX = float(1 << 24)
LNEPS = 1e-5
RMSEPS = 1e-6


class Ev:
    __slots__ = ("eng", "idx", "used", "val", "dma")

    def __init__(self, eng, idx, dma):
        self.eng = eng
        self.idx = idx
        self.used = False
        self.val = None
        self.dma = dma


class Prog:
    ENGS = ("t", "v", "s", "g", "y")

    NQ = 8

    def __init__(self, nc, es):
        self.nc = nc
        self.es = es
        self.steps = {e: [] for e in self.ENGS}
        self.sems = {}
        self.qcount = {}
        for e in self.ENGS:
            self.sems[e] = es.enter_context(nc.semaphore(f"sem_{e}"))
            for i in range(self.NQ):
                self.sems[f"{e}q{i}"] = es.enter_context(nc.semaphore(f"dma_{e}{i}"))
            self.qcount[e] = 0

    def op(self, eng, fn, after=(), dma=False):
        ev = Ev(eng, len(self.steps[eng]), dma)
        after = [a for a in after if a is not None]
        for a in after:
            a.used = True
        if dma:
            k = self.qcount[eng]
            self.qcount[eng] = k + 1
            sem_name = f"{eng}q{k % self.NQ}"
            n_prior = k // self.NQ
            ev.val = (sem_name, 16 * (n_prior + 1))
            ev.used = True
            if n_prior > 0:
                prev = Ev(eng, -1, True)
                prev.val = (sem_name, 16 * n_prior)
                prev.used = True
                after = after + [prev]
        self.steps[eng].append((fn, after, ev, dma))
        return ev

    def qfence(self, eng):
        evs = []
        k = self.qcount[eng]
        for i in range(self.NQ):
            n = (k - i + self.NQ - 1) // self.NQ
            if n > 0:
                ev = Ev(eng, -1, True)
                ev.val = (f"{eng}q{i}", 16 * n)
                ev.used = True
                evs.append(ev)
        return evs

    def dma(self, eng, out, in_, after=(), **kw):
        return self.op(eng, lambda E: E.dma_start(out=out, in_=in_, **kw),
                       after=after, dma=True)

    def finalize(self):
        for e in self.ENGS:
            c = 0
            for (fn, after, ev, dma) in self.steps[e]:
                if not dma and ev.used:
                    c += 1
                    ev.val = (e, c)
        blk = self.es.enter_context(self.nc.Block())
        prog = self

        def emit_for(eng_name):
            def body(E):
                last_wait = {}
                for (fn, after, ev, dma) in prog.steps[eng_name]:
                    for a in after:
                        sem_name, v = a.val
                        if last_wait.get(sem_name, -1) < v:
                            E.wait_ge(prog.sems[sem_name], v)
                            last_wait[sem_name] = v
                    ins = fn(E)
                    if ev.used:
                        sem_name, v = ev.val
                        ins.then_inc(prog.sems[sem_name], 16 if dma else 1)
            return body

        for nm, meth in (("y", blk.sync), ("t", blk.tensor), ("v", blk.vector),
                         ("s", blk.scalar), ("g", blk.gpsimd)):
            if self.steps[nm]:
                meth(emit_for(nm))


class Buf:
    __slots__ = ("wr", "rd")

    def __init__(self):
        self.wr = None
        self.rd = []

    def war(self):
        deps = list(self.rd)
        if self.wr is not None:
            deps.append(self.wr)
        return deps

    def raw(self):
        return [self.wr] if self.wr is not None else []

    def written(self, ev):
        self.wr = ev
        self.rd = []

    def read(self, ev):
        self.rd.append(ev)


def _causal_master():
    kr = np.arange(128)[:, None]
    u = np.arange(896)[None, :]
    return np.where(u >= kr + 384, 0.0, -30000.0).astype(np.float32)


def make_consts():
    c = {}
    c["c_ident_b"] = np.eye(128, dtype=BF16NP)
    c["c_ident_f"] = np.eye(128, dtype=np.float32)
    c["c_ones_col_f"] = np.ones((128, 1), np.float32)
    c["c_triu_b"] = np.triu(np.ones((128, 128), np.float32)).astype(BF16NP)
    c["c_ones_row_b"] = np.ones((1, 128), BF16NP)
    c["c_ones_col_b"] = np.ones((128, 1), BF16NP)
    c["c_ones128_b"] = np.ones((128, 128), BF16NP)
    c["c_ones_row_f"] = np.ones((1, 128), np.float32)
    c["c_iota8"] = np.tile(np.arange(8, dtype=np.float32), (128, 1))
    c["c_iota16"] = np.tile(np.arange(16, dtype=np.float32), (128, 1))
    c["c_iotacol"] = np.arange(128, dtype=np.float32).reshape(128, 1)
    c["c_mask896"] = _causal_master().astype(BF16NP)
    bd4 = np.zeros((128, 4), np.float32)
    for h in range(4):
        bd4[h * 32:(h + 1) * 32, h] = 1.0
    c["c_bd4_b"] = bd4.astype(BF16NP)
    bdrep = np.zeros((128, 128), np.float32)
    for hh in range(4):
        bdrep[hh * 32:(hh + 1) * 32, hh * 32:(hh + 1) * 32] = 1.0
    c["c_bdrep_b"] = bdrep.astype(BF16NP)
    c["c_bd4T_b"] = np.ascontiguousarray(bd4.T).astype(BF16NP)
    sel = np.zeros((16, 16 * 128), np.float32)
    for e in range(16):
        sel[e, e * 128:(e + 1) * 128] = 1.0
    c["c_sel16_b"] = sel.astype(BF16NP)
    return c


def prep_inputs(inp):
    d = {}
    d["p_qkv_wT"] = np.ascontiguousarray(inp["L0_attn_qkv_w"].T)
    d["p_qkv_b"] = np.ascontiguousarray(inp["L0_attn_qkv_b"].reshape(24, 128).T)
    d["p_qkv_b_row"] = np.ascontiguousarray(inp["L0_attn_qkv_b"].reshape(1, 3 * D))
    d["p_proj_wT"] = np.ascontiguousarray(inp["L0_attn_proj_w"].T)
    d["p_proj_b_row"] = np.ascontiguousarray(inp["L0_attn_proj_b"].reshape(1, D))
    d["p_wg"] = np.ascontiguousarray(
        inp["L0_ffn_wg"].reshape(8, 128, 8).transpose(1, 0, 2).reshape(128, 64))
    d["p_w1_0"] = inp["L0_ffn_w1"]
    d["p_b1_0"] = np.ascontiguousarray(inp["L0_ffn_b1"].reshape(8, 32, 128).transpose(0, 2, 1))
    d["p_w2_0"] = inp["L0_ffn_w2"]
    d["p_b2_0"] = np.ascontiguousarray(inp["L0_ffn_b2"].reshape(8, 8, 128).transpose(0, 2, 1))
    perm = []
    for g in range(4):
        for j in range(4):
            perm.extend([(4 * g + j) * 64 + dd for dd in range(32)])
    for g in range(4):
        for j in range(4):
            perm.extend([(4 * g + j) * 64 + dd for dd in range(32, 64)])
    d["p_WqP"] = np.ascontiguousarray(inp["L1_attn_Wq"][:, np.array(perm)])
    d["p_Wkv"] = inp["L1_attn_Wkv"]
    d["p_Wk"] = inp["L1_attn_Wk"]
    d["p_Wv"] = inp["L1_attn_Wv"]
    d["p_qgA"] = np.ascontiguousarray(np.tile(inp["L1_attn_q_g"][:32], 4).reshape(128, 1))
    d["p_kg"] = np.ascontiguousarray(np.tile(inp["L1_attn_k_g"], 4).reshape(128, 1))
    rows = []
    for g in range(4):
        for j in range(4):
            rows.extend([(4 * g + j) * 64 + dd for dd in range(32)])
    d["p_Wo_nz"] = np.ascontiguousarray(inp["L1_attn_Wo"][np.array(rows), :])
    d["p_gate_wT"] = np.ascontiguousarray(inp["L1_ffn_gate_w"].T)
    d["p_ebias_row"] = np.ascontiguousarray(inp["L1_ffn_expert_bias"].reshape(1, 16))
    d["p_w1T_1"] = np.ascontiguousarray(inp["L1_ffn_w1"].transpose(0, 2, 1))
    d["p_w3T_1"] = np.ascontiguousarray(inp["L1_ffn_w3"].transpose(0, 2, 1))
    d["p_w2T_1"] = np.ascontiguousarray(inp["L1_ffn_w2"].transpose(0, 2, 1))
    d["p_sw1T"] = np.ascontiguousarray(inp["L1_ffn_sw1"].T)
    d["p_sw3T"] = np.ascontiguousarray(inp["L1_ffn_sw3"].T)
    d["p_sw2T"] = np.ascontiguousarray(inp["L1_ffn_sw2"].T)
    return d


def build_phaseA():
    nc = bass.Bass("TRN2", target_bir_lowering=False)
    es = ExitStack()
    p = Prog(nc, es)
    SCALE0 = 1.0 / np.sqrt(HD0)

    xb_d = nc.dram_tensor("xb", [T, D], F32, kind="ExternalInput").ap()
    wqkvT_d = nc.dram_tensor("p_qkv_wT", [D, 3 * D], F32, kind="ExternalInput").ap()
    qkvb_d = nc.dram_tensor("p_qkv_b", [128, 24], F32, kind="ExternalInput").ap()
    qkvbrow_d = nc.dram_tensor("p_qkv_b_row", [1, 3 * D], F32, kind="ExternalInput").ap()
    projwT_d = nc.dram_tensor("p_proj_wT", [D, D], F32, kind="ExternalInput").ap()
    projbrow_d = nc.dram_tensor("p_proj_b_row", [1, D], F32, kind="ExternalInput").ap()
    wg_d = nc.dram_tensor("p_wg", [128, 64], F32, kind="ExternalInput").ap()
    identb_d = nc.dram_tensor("c_ident_b", [128, 128], BF16, kind="ExternalInput").ap()
    triu_d = nc.dram_tensor("c_triu_b", [128, 128], BF16, kind="ExternalInput").ap()
    onesrow_d = nc.dram_tensor("c_ones_row_b", [1, 128], BF16, kind="ExternalInput").ap()
    onescol_d = nc.dram_tensor("c_ones_col_b", [128, 1], BF16, kind="ExternalInput").ap()
    ones128_d = nc.dram_tensor("c_ones128_b", [128, 128], BF16, kind="ExternalInput").ap()
    iota8_d = nc.dram_tensor("c_iota8", [128, 8], F32, kind="ExternalInput").ap()

    x1_d = nc.dram_tensor("X1", [T, D], F32, kind="ExternalOutput").ap()
    g1v_d = nc.dram_tensor("G1V", [128, NT], F32, kind="ExternalOutput").ap()
    eix_d = nc.dram_tensor("EIX", [128, NT], F32, kind="ExternalOutput").ap()
    rloc_d = nc.dram_tensor("RLOC", [128, NT], F32, kind="ExternalOutput").ap()
    counts_d = nc.dram_tensor("COUNTS", [1, 8], F32, kind="ExternalOutput").ap()

    khbm = nc.dram_tensor("KHBM", [2 * D, T], BF16, kind="Internal").ap()
    vhbm = nc.dram_tensor("VHBM", [T, D], BF16, kind="Internal").ap()

    sbt = lambda name, shape, dt: es.enter_context(nc.sbuf_tensor(name, shape, dt))
    pst = lambda name, shape, dt: es.enter_context(nc.psum_tensor(name, shape, dt))

    xt = [sbt(f"xt{i}", [128, D], F32) for i in range(2)]
    xbf = [sbt(f"xbf{i}", [128, D], BF16) for i in range(2)]
    xfmc = [sbt(f"xfmc{i}", [128, 8 * 512], BF16) for i in range(2)]
    wqkv = sbt("wqkv", [128, 8 * 3 * D], BF16)
    projw = sbt("projw", [128, 8 * D], BF16)
    qkvb = sbt("qkvb", [128, 24], F32)
    bvrow = sbt("bvrow", [1, 3 * D], F32)
    pbrow = sbt("pbrow", [1, D], F32)
    identb = sbt("identb", [128, 128], BF16)
    triub = sbt("triub", [128, 128], BF16)
    onesrow = sbt("onesrow", [1, 128], BF16)
    onescol = sbt("onescol", [128, 1], BF16)
    ones128 = sbt("ones128", [128, 128], BF16)
    offsb = sbt("offsb", [128, 8], F32)
    rtot = sbt("rtot", [128, 8], F32)
    iota8 = sbt("iota8", [128, 8], F32)
    wgt = sbt("wgt", [128, 64], BF16)
    qfm = sbt("qfm", [128, 8 * 512], BF16)
    kstage = [sbt(f"kstage{i}", [128, 512], BF16) for i in range(2)]
    vstage = [sbt(f"vstage{i}", [128, 1024], BF16) for i in range(2)]
    kb = [sbt(f"kb{i}", [128, 512], BF16) for i in range(2)]
    vb = [sbt(f"vb{i}", [128, 4 * 64], BF16) for i in range(2)]
    pbt = sbt("pbt", [128, D], F32)
    bvr_b = sbt("bvr_b", [1, 3 * D], BF16)
    pbr_b = sbt("pbr_b", [1, D], BF16)
    onesrow_f = sbt("onesrow_f", [1, 128], BF16)

    fm_m = sbt("fm_m", [128, 4], F32)
    fm_l = sbt("fm_l", [128, 4], F32)
    fm_mb = sbt("fm_mb", [128, 4], F32)
    fm_nm = sbt("fm_nm", [128, 4], F32)
    fm_al = sbt("fm_al", [128, 4], F32)
    fm_P = sbt("fm_P", [128, 4], F32)
    fm_f = sbt("fm_f", [128, 4], F32)
    fm_rl = sbt("fm_rl", [128, 4], F32)
    fm_t1 = sbt("fm_t1", [128, 4], F32)
    fm_o = sbt("fm_o", [128, 4 * HD0], F32)
    fm_ob = sbt("fm_ob", [128, 4 * HD0], F32)
    psb = sbt("psb", [128, 4 * 512], BF16)
    pts = sbt("pts", [128, 512], BF16)
    otm = sbt("otm", [128, 4 * D], BF16)
    ofm = sbt("ofm", [128, 8 * 512], BF16)
    atn = sbt("atn", [128, 4 * D], F32)

    ysum = sbt("ysum", [128, 1], F32)
    ysq = sbt("ysq", [128, 1], F32)
    ymu = sbt("ymu", [128, 1], F32)
    yvar = sbt("yvar", [128, 1], F32)
    yrstd = sbt("yrstd", [128, 1], F32)
    epsA = sbt("epsA", [128, 1], F32)
    yscr = sbt("yscr", [128, D], F32)
    ytile = sbt("ytile", [128, D], F32)
    x1t = sbt("x1t", [128, D], F32)
    x1b = sbt("x1b", [128, D], BF16)
    x1fp = sbt("x1fp", [128, 8 * 128], BF16)
    rl = sbt("rl", [128, 8], F32)
    rme = sbt("rme", [128, 1], F32)
    rnm = sbt("rnm", [128, 1], F32)
    rex = sbt("rex", [128, 8], F32)
    rsum = sbt("rsum", [128, 1], F32)
    rmx8 = sbt("rmx8", [128, 8], F32)
    ridx = sbt("ridx", [128, 8], U32)
    roh = sbt("roh", [128, 8], BF16)
    rohf = sbt("rohf", [128, 8], F32)
    rics = sbt("rics", [128, 8], F32)
    rtmp = sbt("rtmp", [128, 8], F32)
    offs = sbt("offs", [1, 8], F32)
    sg1v = sbt("sg1v", [128, NT], F32)
    seix = sbt("seix", [128, NT], F32)
    srloc = sbt("srloc", [128, NT], F32)

    ps_t = [pst(f"ps_t{i}", [128, 128], BF16) for i in range(2)]
    ps_s = [pst(f"ps_s{i}", [128, 512], F32) for i in range(4)]
    ps_a = pst("ps_a", [128, 512], F32)
    ps_b = pst("ps_b", [128, 512], F32)

    bufs = {}

    def bf(obj):
        key = id(obj)
        if key not in bufs:
            bufs[key] = Buf()
        return bufs[key]

    ev_w = []
    for c in range(8):
        ev_w.append(p.dma("g", wqkv[:, c * 3072:(c + 1) * 3072],
                          wqkvT_d[c * 128:(c + 1) * 128, :]))
        ev_w.append(p.dma("g", projw[:, c * 1024:(c + 1) * 1024],
                          projwT_d[c * 128:(c + 1) * 128, :]))
    e_qkvb = p.dma("y", qkvb[:], qkvb_d)
    e_bvrow = p.dma("y", bvrow[:], qkvbrow_d)
    e_pbrow = p.dma("y", pbrow[:], projbrow_d)
    e_ident = p.dma("y", identb[:], identb_d)
    e_triu = p.dma("y", triub[:], triu_d)
    e_ones = p.dma("y", onesrow[:], onesrow_d)
    e_onesc = p.dma("y", onescol[:], onescol_d)
    e_ones128 = p.dma("y", ones128[:], ones128_d)
    e_iota = p.dma("y", iota8[:], iota8_d)
    e_wgt = p.dma("g", wgt[:], wg_d)
    e_bvb = p.op("v", lambda E: E.tensor_copy(bvr_b[:], bvrow[:]), after=[e_bvrow])
    e_pbb = p.op("v", lambda E: E.tensor_copy(pbr_b[:], pbrow[:]), after=[e_pbrow])
    e_eps = p.op("v", lambda E: E.memset(epsA[:], LNEPS))
    e_offs0 = p.op("v", lambda E: E.memset(offsb[:], 0.0))
    bf(offsb).written(e_offs0)

    e_b1 = p.op("t", lambda E: E.matmul(ps_a[:], onesrow[:], pbr_b[0:1, 0:512],
                                        start=True, stop=True), after=[e_ones, e_pbb])
    e_b2 = p.op("t", lambda E: E.matmul(ps_b[:], onesrow[:], pbr_b[0:1, 512:1024],
                                        start=True, stop=True), after=[e_ones, e_pbb])
    e_pb1 = p.op("v", lambda E: E.tensor_copy(pbt[:, 0:512], ps_a[:]), after=[e_b1])
    e_pb2 = p.op("v", lambda E: E.tensor_copy(pbt[:, 512:1024], ps_b[:]), after=[e_b2])
    bf(pbt).written(e_pb2)
    bf(ps_a).read(e_pb1)
    bf(ps_b).read(e_pb2)

    for tc in range(8):
        xc = xfmc[tc % 2]
        for i4 in range(4):
            tt = tc * 4 + i4
            lb = tt % 2
            e_ld = p.dma("y", xt[lb][:], xb_d[tt * 128:(tt + 1) * 128, :],
                         after=bf(xt[lb]).war())
            bf(xt[lb]).written(e_ld)
            e_cast = p.op("v", lambda E, o=xbf[lb], i=xt[lb]: E.tensor_copy(o[:], i[:]),
                          after=bf(xbf[lb]).war() + [e_ld])
            bf(xt[lb]).read(e_cast)
            bf(xbf[lb]).written(e_cast)
            for c in range(8):
                pt = ps_t[c % 2]
                e_tp = p.op("t", lambda E, o=pt, i=xbf[lb], cc=c:
                            E.transpose(o[:], i[:, cc * 128:(cc + 1) * 128], identb[:]),
                            after=bf(pt).war() + [e_cast, e_ident] + (bf(xc).war() if i4 == 0 and c == 0 else []))
                bf(xbf[lb]).read(e_tp)
                bf(pt).written(e_tp)
                e_ev = p.op("v", lambda E, o=xc, i=pt, cc=c, ii=i4:
                            E.tensor_copy(o[:, cc * 512 + ii * 128: cc * 512 + (ii + 1) * 128], i[:]),
                            after=[e_tp])
                bf(pt).read(e_ev)
                bf(xc).read(e_ev)
        e_xfm_done = e_ev
        bf(xc).written(e_xfm_done)

        for kc in range(8):
            pse = ps_s[kc % 2]
            stg = kstage[kc % 2]
            last = None
            for c in range(8):
                last = p.op("t", lambda E, o=pse, cc=c, kk=kc, xcc=xc:
                            E.matmul(o[:], wqkv[:, cc * 3072 + 1024 + kk * 128: cc * 3072 + 1024 + (kk + 1) * 128],
                                     xcc[:, cc * 512:(cc + 1) * 512],
                                     start=(cc == 0), stop=(cc == 7)),
                            after=(bf(pse).war() + [e_xfm_done] + ev_w) if c == 0 else [])
                if c == 0:
                    ev_w = []
            bf(xc).read(last)
            bf(pse).written(last)
            e_k = p.op("v", lambda E, o=stg, i=pse, kk=kc:
                       E.tensor_scalar(o[:], i[:], qkvb[:, 8 + kk:9 + kk], None, ALU.add),
                       after=bf(stg).war() + [last, e_qkvb])
            bf(pse).read(e_k)
            bf(stg).written(e_k)
            e_kd = p.dma("y", khbm[kc * 128:(kc + 1) * 128, tc * 512:(tc + 1) * 512], stg[:],
                         after=[e_k])
            bf(stg).read(e_kd)

        for i4 in range(4):
            stg = vstage[i4 % 2]
            evh = []
            for half in range(2):
                pse = ps_s[2 + half]
                last = None
                for c in range(8):
                    last = p.op("t", lambda E, o=pse, cc=c, hh=half, ii=i4, xcc=xc:
                                E.matmul(o[:], xcc[:, cc * 512 + ii * 128: cc * 512 + (ii + 1) * 128],
                                         wqkv[:, cc * 3072 + 2048 + hh * 512: cc * 3072 + 2048 + (hh + 1) * 512],
                                         start=(cc == 0), stop=False),
                                after=(bf(pse).war() + [e_xfm_done]) if c == 0 else [])
                last = p.op("t", lambda E, o=pse, hh=half:
                            E.matmul(o[:], onesrow[:], bvr_b[0:1, 2048 + hh * 512: 2048 + (hh + 1) * 512],
                                     start=False, stop=True),
                            after=[e_ones, e_bvb])
                bf(xc).read(last)
                bf(pse).written(last)
                e_v = p.op("v", lambda E, o=stg, i=pse, hh=half:
                           E.tensor_copy(o[:, hh * 512:(hh + 1) * 512], i[:]),
                           after=(bf(stg).war() if half == 0 else []) + [last])
                bf(pse).read(e_v)
                evh.append(e_v)
            bf(stg).written(evh[-1])
            tt = tc * 4 + i4
            e_vd = p.dma("y", vhbm[tt * 128:(tt + 1) * 128, :], stg[:], after=evh)
            bf(stg).read(e_vd)

        if tc == 7:
            for qc in range(8):
                pse = ps_s[qc % 2]
                last = None
                for c in range(8):
                    last = p.op("t", lambda E, o=pse, cc=c, qq=qc, xcc=xc:
                                E.matmul(o[:], wqkv[:, cc * 3072 + qq * 128: cc * 3072 + (qq + 1) * 128],
                                         xcc[:, cc * 512:(cc + 1) * 512],
                                         start=(cc == 0), stop=(cc == 7)),
                                after=(bf(pse).war() + [e_xfm_done]) if c == 0 else [])
                bf(xc).read(last)
                bf(pse).written(last)
                e_q = p.op("v", lambda E, o=qfm, i=pse, qq=qc:
                           E.tensor_scalar(o[:, qq * 512:(qq + 1) * 512], i[:],
                                           qkvb[:, qq:qq + 1], None, ALU.add),
                           after=[last, e_qkvb])
                bf(pse).read(e_q)
                bf(qfm).read(e_q)
            bf(qfm).written(e_q)

    e_qdone = bf(qfm).wr
    yf_a2 = p.qfence("y")
    for h in range(16):
        qc = h // 2
        qb0 = 64 * (h % 2)
        e_minit = None
        for j in range(8):
            lb = j % 2
            e_lk = p.dma("s", kb[lb][qb0:qb0 + 64, :],
                         khbm[h * 64:(h + 1) * 64, j * 512:(j + 1) * 512],
                         after=bf(kb[lb]).war() + yf_a2)
            bf(kb[lb]).written(e_lk)
            e_s = []
            for qt in range(4):
                pse = ps_s[qt]
                ee = p.op("t", lambda E, o=pse, qq=qc, q0=qb0, qqt=qt, kk=kb[lb]:
                          E.matmul(o[:], qfm[q0:q0 + 64, qq * 512 + qqt * 128: qq * 512 + (qqt + 1) * 128],
                                   kk[q0:q0 + 64, :], start=True, stop=True),
                          after=bf(pse).war() + [e_lk, e_qdone])
                bf(kb[lb]).read(ee)
                bf(pse).written(ee)
                e_s.append(ee)
            e_mb = []
            for qt in range(4):
                ee = p.op("v", lambda E, i=ps_s[qt], qqt=qt:
                          E.reduce_max(fm_mb[:, qqt:qqt + 1], i[:], axis=AX.X),
                          after=[e_s[qt]] + (bf(fm_mb).war() if qt == 0 else []))
                bf(ps_s[qt]).read(ee)
                e_mb.append(ee)
            bf(fm_mb).written(e_mb[-1])
            if j == 0:
                e_mn = p.op("v", lambda E: E.tensor_scalar(fm_m[:], fm_mb[:], SCALE0, None, ALU.mult),
                            after=e_mb + bf(fm_m).war())
            else:
                e_sc = p.op("v", lambda E: E.tensor_scalar(fm_mb[:], fm_mb[:], SCALE0, None, ALU.mult),
                            after=e_mb)
                e_mn0 = p.op("v", lambda E: E.tensor_max(fm_t1[:], fm_m[:], fm_mb[:]),
                             after=[e_sc] + bf(fm_t1).war() + bf(fm_m).raw())
                e_d = p.op("v", lambda E: E.tensor_sub(fm_al[:], fm_m[:], fm_t1[:]),
                           after=[e_mn0] + bf(fm_al).war())
                e_al = p.op("s", lambda E: E.activation(fm_al[:], fm_al[:], AF.Exp), after=[e_d])
                bf(fm_al).written(e_al)
                e_mn = p.op("v", lambda E: E.tensor_copy(fm_m[:], fm_t1[:]), after=[e_mn0, e_d])
            bf(fm_m).written(e_mn)
            e_nm = p.op("v", lambda E: E.tensor_scalar(fm_nm[:], fm_m[:], -1.0, None, ALU.mult),
                        after=[e_mn] + bf(fm_nm).war())
            bf(fm_nm).written(e_nm)
            e_p = []
            for qt in range(4):
                ee = p.op("s", lambda E, i=ps_s[qt], qqt=qt:
                          E.activation(psb[:, qqt * 512:(qqt + 1) * 512], i[:], AF.Exp,
                                       bias=fm_nm[:, qqt:qqt + 1], scale=SCALE0,
                                       accum_out=fm_P[:, qqt:qqt + 1]),
                          after=[e_nm, e_s[qt]] + (bf(psb).war() + bf(fm_P).war() if qt == 0 else []))
                bf(ps_s[qt]).read(ee)
                e_p.append(ee)
            bf(psb).written(e_p[-1])
            bf(fm_P).written(e_p[-1])
            if j == 0:
                e_l = p.op("v", lambda E: E.tensor_copy(fm_l[:], fm_P[:]),
                           after=[e_p[-1]] + bf(fm_l).war())
                bf(fm_l).written(e_l)
                continue
            e_t1 = p.op("v", lambda E: E.tensor_mul(fm_t1[:], fm_l[:], fm_al[:]),
                        after=[e_p[-1], e_al] + bf(fm_t1).war() + bf(fm_l).raw())
            e_ln = p.op("v", lambda E: E.tensor_add(fm_t1[:], fm_t1[:], fm_P[:]), after=[e_t1])
            e_rc = p.op("v", lambda E: E.reciprocal(fm_rl[:], fm_t1[:]),
                        after=[e_ln] + bf(fm_rl).war())
            e_f = p.op("v", lambda E: E.tensor_mul(fm_f[:], fm_l[:], fm_rl[:]),
                       after=[e_rc] + bf(fm_f).war())
            bf(fm_f).written(e_f)
            e_lu = p.op("v", lambda E: E.tensor_copy(fm_l[:], fm_t1[:]), after=[e_f])
            bf(fm_l).written(e_lu)
            bf(fm_t1).written(e_lu)
            e_lv = p.dma("s", vb[lb][:].rearrange("p (a d) -> p a d", a=4),
                         vhbm[j * 512:(j + 1) * 512, h * 64:(h + 1) * 64]
                         .rearrange("(a p) d -> p a d", p=128),
                         after=bf(vb[lb]).war() + yf_a2)
            bf(vb[lb]).written(e_lv)
            for qt in range(4):
                pso = ps_b
                e_obmm = []
                for kt in range(4):
                    ptp = ps_t[kt % 2]
                    e_tp = p.op("t", lambda E, o=ptp, qqt=qt, kk=kt:
                                E.transpose(o[:], psb[:, qqt * 512 + kk * 128: qqt * 512 + (kk + 1) * 128],
                                            identb[:]),
                                after=bf(ptp).war() + ([e_p[-1]] if kt == 0 and qt == 0 else []))
                    bf(psb).read(e_tp)
                    bf(ptp).written(e_tp)
                    e_pe = p.op("v", lambda E, i=ptp, kk=kt:
                                E.tensor_copy(pts[:, kk * 128:(kk + 1) * 128], i[:]),
                                after=[e_tp] + (bf(pts).war() if kt == 0 else []))
                    bf(ptp).read(e_pe)
                    e_mm = p.op("t", lambda E, o=pso, kk=kt, vv=vb[lb]:
                                E.matmul(o[0:128, 0:64], pts[:, kk * 128:(kk + 1) * 128],
                                         vv[:, kk * 64:(kk + 1) * 64],
                                         start=(kk == 0), stop=(kk == 3)),
                                after=[e_pe, e_lv] + (bf(pso).war() if kt == 0 else []))
                    bf(vb[lb]).read(e_mm)
                    e_obmm.append(e_mm)
                bf(pts).written(e_pe)
                bf(pso).written(e_obmm[-1])
                e_ob = p.op("v", lambda E, qqt=qt:
                            E.tensor_copy(fm_ob[:, qqt * 64:(qqt + 1) * 64], ps_b[0:128, 0:64]),
                            after=[e_obmm[-1]] + bf(fm_ob).war())
                bf(ps_b).read(e_ob)
                bf(fm_ob).written(e_ob)
                if j == 1:
                    e_ou = p.op("v", lambda E, qqt=qt:
                                E.tensor_scalar(fm_o[:, qqt * 64:(qqt + 1) * 64],
                                                fm_ob[:, qqt * 64:(qqt + 1) * 64],
                                                fm_f[:, qqt:qqt + 1], None, ALU.mult),
                                after=[e_ob, e_f] + bf(fm_o).war())
                else:
                    e_d1 = p.op("v", lambda E, qqt=qt:
                                E.tensor_sub(fm_ob[:, qqt * 64:(qqt + 1) * 64],
                                             fm_ob[:, qqt * 64:(qqt + 1) * 64],
                                             fm_o[:, qqt * 64:(qqt + 1) * 64]),
                                after=[e_ob] + bf(fm_o).raw())
                    e_ou = p.op("v", lambda E, qqt=qt:
                                E.scalar_tensor_tensor(fm_o[:, qqt * 64:(qqt + 1) * 64],
                                                       fm_ob[:, qqt * 64:(qqt + 1) * 64],
                                                       fm_f[:, qqt:qqt + 1],
                                                       fm_o[:, qqt * 64:(qqt + 1) * 64],
                                                       ALU.mult, ALU.add),
                                after=[e_d1, e_f])
                bf(fm_o).written(e_ou)
        e_rc2 = p.op("v", lambda E: E.reciprocal(fm_rl[:], fm_l[:]),
                     after=bf(fm_l).raw() + bf(fm_rl).war() + [e_ou])
        bf(fm_rl).written(e_rc2)
        for qt in range(4):
            e_fin = p.op("v", lambda E, qqt=qt, hh=h:
                         E.tensor_scalar(otm[:, qqt * 1024 + hh * 64: qqt * 1024 + (hh + 1) * 64],
                                         fm_o[:, qqt * 64:(qqt + 1) * 64],
                                         fm_rl[:, qqt:qqt + 1], None, ALU.mult),
                         after=[e_rc2])
            bf(fm_o).read(e_fin)
            bf(otm).read(e_fin)
    bf(otm).written(e_fin)

    e_otm = bf(otm).wr
    for qt in range(4):
        for c in range(8):
            ptp = ps_t[c % 2]
            e_tp = p.op("t", lambda E, o=ptp, qqt=qt, cc=c:
                        E.transpose(o[:], otm[:, qqt * 1024 + cc * 128: qqt * 1024 + (cc + 1) * 128],
                                    identb[:]),
                        after=bf(ptp).war() + [e_otm])
            bf(otm).read(e_tp)
            bf(ptp).written(e_tp)
            e_ev = p.op("v", lambda E, i=ptp, qqt=qt, cc=c:
                        E.tensor_copy(ofm[:, cc * 512 + qqt * 128: cc * 512 + (qqt + 1) * 128], i[:]),
                        after=[e_tp])
            bf(ptp).read(e_ev)
            bf(ofm).read(e_ev)
    bf(ofm).written(e_ev)
    e_ofm = bf(ofm).wr
    for qt in range(4):
        for half in range(2):
            pse = ps_s[half]
            last = None
            for c in range(8):
                last = p.op("t", lambda E, o=pse, cc=c, qqt=qt, hh=half:
                            E.matmul(o[:], ofm[:, cc * 512 + qqt * 128: cc * 512 + (qqt + 1) * 128],
                                     projw[:, cc * 1024 + hh * 512: cc * 1024 + (hh + 1) * 512],
                                     start=(cc == 0), stop=(cc == 7)),
                            after=(bf(pse).war() + [e_ofm]) if c == 0 else [])
            bf(ofm).read(last)
            bf(pse).written(last)
            e_at = p.op("v", lambda E, i=pse, qqt=qt, hh=half:
                        E.tensor_copy(atn[:, qqt * 1024 + hh * 512: qqt * 1024 + (hh + 1) * 512], i[:]),
                        after=[last])
            bf(pse).read(e_at)
            bf(atn).read(e_at)
    bf(atn).written(e_at)

    e_cnt_prev = bf(offs).wr
    for tt in range(NT):
        lb = tt % 2
        e_ld = p.dma("y", xt[lb][:], xb_d[tt * 128:(tt + 1) * 128, :], after=bf(xt[lb]).war())
        bf(xt[lb]).written(e_ld)
        if tt < 28:
            e_y = p.op("v", lambda E, xx=xt[lb]: E.tensor_add(ytile[:], xx[:], pbt[:]),
                       after=[e_ld] + bf(pbt).raw() + bf(ytile).war())
        else:
            qt = tt - 28
            e_y0 = p.op("v", lambda E, qqt=qt, xx=xt[lb]: E.tensor_add(ytile[:], xx[:],
                                                            atn[:, qqt * 1024:(qqt + 1) * 1024]),
                        after=[e_ld] + bf(atn).raw() + bf(ytile).war())
            e_y = p.op("v", lambda E: E.tensor_add(ytile[:], ytile[:], pbt[:]),
                       after=[e_y0] + bf(pbt).raw())
        bf(xt[lb]).read(e_y)
        bf(ytile).written(e_y)
        e_s1 = p.op("v", lambda E: E.reduce_sum(ysum[:], ytile[:], axis=AX.X),
                    after=[e_y] + bf(ysum).war())
        bf(ysum).written(e_s1)
        e_s2 = p.op("s", lambda E: E.activation(yscr[:], ytile[:], AF.Square, accum_out=ysq[:]),
                    after=[e_y] + bf(ysq).war() + bf(yscr).war())
        bf(ysq).written(e_s2)
        bf(yscr).written(e_s2)
        e_mu = p.op("v", lambda E: E.tensor_scalar(ymu[:], ysum[:], 1.0 / D, None, ALU.mult),
                    after=[e_s1] + bf(ymu).war())
        bf(ymu).written(e_mu)
        e_m2 = p.op("v", lambda E: E.tensor_scalar(yvar[:], ysq[:], 1.0 / D, None, ALU.mult),
                    after=[e_s2] + bf(yvar).war())
        e_mm = p.op("v", lambda E: E.tensor_mul(ysum[:], ymu[:], ymu[:]), after=[e_mu, e_m2])
        e_vv = p.op("v", lambda E: E.tensor_sub(yvar[:], yvar[:], ysum[:]), after=[e_mm])
        bf(ysum).written(e_vv)
        e_l1 = p.op("s", lambda E: E.activation(yrstd[:], yvar[:], AF.Ln, bias=epsA[:]),
                    after=[e_vv, e_eps] + bf(yrstd).war())
        e_l2 = p.op("s", lambda E: E.activation(yrstd[:], yrstd[:], AF.Exp, scale=-0.5), after=[e_l1])
        bf(yvar).written(e_l2)
        bf(yrstd).written(e_l2)
        e_x1 = p.op("v", lambda E: E.tensor_scalar(x1t[:], ytile[:], ymu[:], yrstd[:],
                                                   ALU.subtract, ALU.mult),
                    after=[e_l2, e_mu] + bf(x1t).war())
        bf(ytile).read(e_x1)
        bf(x1t).written(e_x1)
        e_st = p.dma("y", x1_d[tt * 128:(tt + 1) * 128, :], x1t[:], after=[e_x1])
        bf(x1t).read(e_st)
        e_cb = p.op("v", lambda E: E.tensor_copy(x1b[:], x1t[:]), after=[e_x1] + bf(x1b).war())
        bf(x1t).read(e_cb)
        bf(x1b).written(e_cb)
        for c in range(8):
            ptp = ps_t[c % 2]
            e_tp = p.op("t", lambda E, o=ptp, cc=c:
                        E.transpose(o[:], x1b[:, cc * 128:(cc + 1) * 128], identb[:]),
                        after=bf(ptp).war() + [e_cb])
            bf(x1b).read(e_tp)
            bf(ptp).written(e_tp)
            e_pe = p.op("v", lambda E, i=ptp, cc=c:
                        E.tensor_copy(x1fp[:, cc * 128:(cc + 1) * 128], i[:]),
                        after=[e_tp] + (bf(x1fp).war() if c == 0 else []))
            bf(ptp).read(e_pe)
        bf(x1fp).written(e_pe)
        psr = ps_a
        last = None
        for c in range(8):
            last = p.op("t", lambda E, o=psr, cc=c:
                        E.matmul(o[0:128, 0:8], x1fp[:, cc * 128:(cc + 1) * 128],
                                 wgt[:, cc * 8:(cc + 1) * 8],
                                 start=(cc == 0), stop=(cc == 7)),
                        after=(bf(psr).war() + [bf(x1fp).wr, e_wgt]) if c == 0 else [])
        bf(x1fp).read(last)
        bf(psr).written(last)
        e_lg = p.op("v", lambda E: E.tensor_copy(rl[:], psr[0:128, 0:8]),
                    after=[last] + bf(rl).war())
        bf(psr).read(e_lg)
        bf(rl).written(e_lg)
        e_rm = p.op("v", lambda E: E.reduce_max(rme[:], rl[:], axis=AX.X, negate=True),
                    after=[e_lg] + bf(rme).war())
        bf(rme).written(e_rm)
        e_ex = p.op("s", lambda E: E.activation(rex[:], rl[:], AF.Exp, bias=rme[:],
                                                accum_out=rsum[:]),
                    after=[e_rm] + bf(rex).war() + bf(rsum).war())
        bf(rex).written(e_ex)
        bf(rsum).written(e_ex)
        e_g1 = p.op("v", lambda E, ttt=tt: E.reciprocal(sg1v[:, ttt:ttt + 1], rsum[:]),
                    after=[e_ex])
        bf(rsum).read(e_g1)
        bf(sg1v).read(e_g1)
        e_mx0 = p.op("v", lambda E: E.memset(rmx8[:], 0.0), after=bf(rmx8).war())
        e_mx1 = p.op("v", lambda E: E.tensor_scalar(rmx8[:], rmx8[:], rme[:], None,
                                                    ALU.subtract),
                     after=[e_mx0, e_rm])
        bf(rmx8).written(e_mx1)
        e_mi = p.op("v", lambda E: E.max_index(ridx[:], rmx8[:], rl[:]),
                    after=[e_mx1, e_lg] + bf(ridx).war())
        bf(rl).read(e_mi)
        bf(ridx).written(e_mi)
        e_if = p.op("v", lambda E, ttt=tt: E.tensor_copy(seix[:, ttt:ttt + 1], ridx[:, 0:1]),
                    after=[e_mi])
        bf(seix).read(e_if)
        e_oh = p.op("v", lambda E, ttt=tt: E.tensor_scalar(roh[:], iota8[:], seix[:, ttt:ttt + 1],
                                                           None, ALU.is_equal),
                    after=[e_if, e_iota] + bf(roh).war())
        bf(roh).written(e_oh)
        e_ic = p.op("t", lambda E: E.matmul(ps_b[0:128, 0:8], triub[:], roh[:],
                                            start=True, stop=True),
                    after=bf(ps_b).war() + [e_oh, e_triu])
        bf(roh).read(e_ic)
        bf(ps_b).written(e_ic)
        e_ics = p.op("v", lambda E: E.tensor_add(rtot[:], ps_b[0:128, 0:8], offsb[:]),
                     after=[e_ic, bf(offsb).wr] + bf(rtot).war())
        bf(ps_b).read(e_ics)
        bf(offsb).read(e_ics)
        bf(rtot).written(e_ics)
        e_ohf = p.op("v", lambda E: E.tensor_copy(rohf[:], roh[:]), after=[e_oh] + bf(rohf).war())
        bf(roh).read(e_ohf)
        bf(rohf).written(e_ohf)
        e_t = p.op("v", lambda E: E.tensor_mul(rtmp[:], rohf[:], rtot[:]),
                   after=[e_ohf, e_ics] + bf(rtmp).war())
        bf(rtmp).written(e_t)
        e_rk = p.op("v", lambda E: E.reduce_sum(rme[:], rtmp[:], axis=AX.X),
                    after=[e_t] + bf(rme).war())
        bf(rtmp).read(e_rk)
        bf(rme).written(e_rk)
        e_rk2 = p.op("v", lambda E, ttt=tt: E.tensor_scalar(srloc[:, ttt:ttt + 1], rme[:],
                                                            -1.0, None, ALU.add),
                     after=[e_rk])
        bf(rme).read(e_rk2)
        bf(srloc).read(e_rk2)
        e_cs = p.op("t", lambda E: E.matmul(ps_b[0:128, 0:8], ones128[:], roh[:],
                                            start=True, stop=True),
                    after=bf(ps_b).war() + [e_oh, e_ones128] + [e_ics])
        bf(roh).read(e_cs)
        bf(ps_b).written(e_cs)
        e_of = p.op("v", lambda E: E.tensor_add(offsb[:], offsb[:], ps_b[0:128, 0:8]),
                    after=[e_cs] + bf(offsb).war())
        bf(ps_b).read(e_of)
        bf(offsb).written(e_of)
    e_cd = p.dma("y", counts_d, offsb[0:1, :], after=[bf(offsb).wr])
    e_g1d = p.dma("y", g1v_d, sg1v[:], after=bf(sg1v).rd)
    e_exd = p.dma("y", eix_d, seix[:], after=bf(seix).rd)
    e_rld = p.dma("y", rloc_d, srloc[:], after=bf(srloc).rd)
    e_fin = p.op("y", lambda E: E.nop(),
                 after=p.qfence("y") + p.qfence("g"))

    p.finalize()
    es.close()
    return nc


def build_phaseB():
    nc = bass.Bass("TRN2", target_bir_lowering=False)
    es = ExitStack()
    p = Prog(nc, es)

    x1_d = nc.dram_tensor("X1", [T + 1, D], F32, kind="ExternalInput").ap()
    g1v_d = nc.dram_tensor("G1V", [128, NT], F32, kind="ExternalInput").ap()
    eix_d = nc.dram_tensor("EIX", [128, NT], F32, kind="ExternalInput").ap()
    rloc_d = nc.dram_tensor("RLOC", [128, NT], F32, kind="ExternalInput").ap()
    cnts_d = nc.dram_tensor("CALL", [8, 8], F32, kind="ExternalInput").ap()
    pmask_d = nc.dram_tensor("PMASK", [8, 1], F32, kind="ExternalInput").ap()
    w1_d = nc.dram_tensor("p_w1_0", [E0, D, 4096], F32, kind="ExternalInput").ap()
    b1_d = nc.dram_tensor("p_b1_0", [E0, 128, 32], F32, kind="ExternalInput").ap()
    w2_d = nc.dram_tensor("p_w2_0", [E0, 4096, D], F32, kind="ExternalInput").ap()
    b2_d = nc.dram_tensor("p_b2_0", [E0, 128, 8], F32, kind="ExternalInput").ap()
    identb_d = nc.dram_tensor("c_ident_b", [128, 128], BF16, kind="ExternalInput").ap()
    iota8_d = nc.dram_tensor("c_iota8", [128, 8], F32, kind="ExternalInput").ap()
    iotacol_d = nc.dram_tensor("c_iotacol", [128, 1], F32, kind="ExternalInput").ap()
    onesrowf_d = nc.dram_tensor("c_ones_row_f", [1, 128], F32, kind="ExternalInput").ap()

    x2_d = nc.dram_tensor("X2", [T, D], F32, kind="ExternalOutput").ap()

    ttab = nc.dram_tensor("TTAB", [128, 97], I32, kind="Internal").ap()
    eout = nc.dram_tensor("EOUT", [NSLOT + 1, D], BF16, kind="Internal").ap()

    sbt = lambda name, shape, dt: es.enter_context(nc.sbuf_tensor(name, shape, dt))
    pst = lambda name, shape, dt: es.enter_context(nc.psum_tensor(name, shape, dt))

    sg1 = sbt("sg1", [128, NT], F32)
    sei = sbt("sei", [128, NT], F32)
    srl = sbt("srl", [128, NT], F32)
    sg1c = sbt("sg1c", [128, NT], F32)
    sslot = sbt("sslot", [128, NT], I32)
    cnts = sbt("cnts", [8, 8], F32)
    pmask = sbt("pmask", [8, 1], F32)
    prefb = sbt("prefb", [128, 8], F32)
    c8row = sbt("c8row", [1, 8], F32)
    ones1f = sbt("ones1f", [1, 128], F32)
    identb = sbt("identb", [128, 128], BF16)
    iota8 = sbt("iota8", [128, 8], F32)
    iotacol = sbt("iotacol", [128, 1], F32)
    bigtab = sbt("bigtab", [128, 97], I32)
    oh8 = sbt("oh8", [128, 8], F32)
    t8 = sbt("t8", [128, 8], F32)
    c1 = sbt("c1", [128, 1], F32)
    c2 = sbt("c2", [128, 1], F32)
    c3 = sbt("c3", [128, 1], F32)
    c4 = sbt("c4", [128, 1], F32)
    fidx = sbt("fidx", [128, 1], I32)
    ci1 = sbt("ci1", [128, 1], I32)
    ci2 = sbt("ci2", [128, 1], I32)
    ci3 = sbt("ci3", [128, 1], I32)
    ci4 = sbt("ci4", [128, 1], I32)
    gidx = [sbt(f"gidx{i}", [128, 1], I32) for i in range(2)]
    midx = sbt("midx", [128, 1], I32)
    tokid = sbt("tokid", [128, 1], I32)
    tsb = sbt("tsb", [128, 97], I32)
    b1s = sbt("b1s", [128, 32], F32)
    b2s = sbt("b2s", [128, 8], F32)
    w1s = [sbt(f"w1s{i}", [128, 8 * 128], BF16) for i in range(2)]
    w2s = [sbt(f"w2s{i}", [128, 32 * 128], BF16) for i in range(2)]
    xgf = [sbt(f"xgf{i}", [128, D], F32) for i in range(2)]
    xgb = [sbt(f"xgb{i}", [128, D], BF16) for i in range(2)]
    xgfm = sbt("xgfm", [128, 8 * 512], BF16)
    hfm = sbt("hfm", [128, 32 * 512], BF16)
    yfm = sbt("yfm", [128, 8 * 512], BF16)
    ytm = sbt("ytm", [128, D], BF16)
    x1t = [sbt(f"x1tB{i}", [128, D], F32) for i in range(2)]
    mg = sbt("mg", [128, D], BF16)
    ytile = sbt("ytileB", [128, D], F32)
    x2t = sbt("x2t", [128, D], F32)
    ysum = sbt("ysumB", [128, 1], F32)
    ysq = sbt("ysqB", [128, 1], F32)
    ymu = sbt("ymuB", [128, 1], F32)
    yvar = sbt("yvarB", [128, 1], F32)
    yrstd = sbt("yrstdB", [128, 1], F32)
    epsB = sbt("epsB", [128, 1], F32)
    yscr = sbt("yscrB", [128, D], F32)

    ps_t = [pst(f"ps_tB{i}", [128, 128], BF16) for i in range(2)]
    ps_s = [pst(f"ps_sB{i}", [128, 512], F32) for i in range(4)]
    ps_p = pst("ps_pB", [128, 8], F32)

    bufs = {}

    def bf(obj):
        key = id(obj)
        if key not in bufs:
            bufs[key] = Buf()
        return bufs[key]

    e_g1 = p.dma("y", sg1[:], g1v_d)
    e_ei = p.dma("y", sei[:], eix_d)
    e_rl = p.dma("y", srl[:], rloc_d)
    e_cn = p.dma("y", cnts[:], cnts_d)
    e_pm = p.dma("y", pmask[:], pmask_d)
    e_id = p.dma("y", identb[:], identb_d)
    e_i8 = p.dma("y", iota8[:], iota8_d)
    e_ic = p.dma("y", iotacol[:], iotacol_d)
    e_o1f = p.dma("y", ones1f[:], onesrowf_d)
    e_eps = p.op("v", lambda E: E.memset(epsB[:], LNEPS))
    e_big = p.op("v", lambda E: E.memset(bigtab[:], T))
    e_ti = p.dma("y", ttab, bigtab[:], after=[e_big])
    bf(bigtab).read(e_ti)
    zrow = sbt("zrow", [1, D], BF16)
    e_zr = p.op("v", lambda E: E.memset(zrow[:], 0.0))
    e_ze = p.dma("y", eout[NSLOT:NSLOT + 1, :], zrow[:], after=[e_zr])
    bf(zrow).read(e_ze)

    e_pf = p.op("t", lambda E: E.matmul(ps_p[0:1, 0:8], pmask[:], cnts[:],
                                        start=True, stop=True), after=[e_cn, e_pm])
    bf(ps_p).written(e_pf)
    e_pfc = p.op("v", lambda E: E.tensor_copy(c8row[0:1, :], ps_p[0:1, 0:8]), after=[e_pf])
    bf(ps_p).read(e_pfc)
    e_pf2 = p.op("t", lambda E: E.matmul(ps_p[0:128, 0:8], ones1f[:], c8row[0:1, :],
                                         start=True, stop=True),
                 after=[e_pfc, e_o1f] + bf(ps_p).war())
    bf(ps_p).written(e_pf2)
    e_pfb = p.op("v", lambda E: E.tensor_copy(prefb[:], ps_p[0:128, 0:8]), after=[e_pf2])
    bf(ps_p).read(e_pfb)
    bf(prefb).written(e_pfb)

    for tt in range(NT):
        e_oh = p.op("v", lambda E, ttt=tt: E.tensor_scalar(oh8[:], iota8[:], sei[:, ttt:ttt + 1],
                                                           None, ALU.is_equal),
                    after=[e_i8, e_ei] + bf(oh8).war())
        bf(oh8).written(e_oh)
        e_t8 = p.op("v", lambda E: E.tensor_mul(t8[:], oh8[:], prefb[:]),
                    after=[e_oh, e_pfb] + bf(t8).war())
        bf(t8).written(e_t8)
        e_px = p.op("v", lambda E: E.reduce_sum(c1[:], t8[:], axis=AX.X),
                    after=[e_t8] + bf(c1).war())
        bf(t8).read(e_px)
        bf(c1).written(e_px)
        e_gp = p.op("v", lambda E, ttt=tt: E.tensor_add(c2[:], srl[:, ttt:ttt + 1], c1[:]),
                    after=[e_px, e_rl] + bf(c2).war())
        bf(c2).written(e_gp)
        e_kp = p.op("v", lambda E: E.tensor_scalar(c3[:], c2[:], float(CAP0), None, ALU.is_lt),
                    after=[e_gp] + bf(c3).war())
        bf(c3).written(e_kp)
        e_g1c = p.op("v", lambda E, ttt=tt: E.tensor_mul(sg1c[:, ttt:ttt + 1], sg1[:, ttt:ttt + 1], c3[:]),
                     after=[e_kp, e_g1])
        bf(sg1c).read(e_g1c)
        e_ov = p.op("v", lambda E, ttt=tt: E.tensor_scalar(c4[:], srl[:, ttt:ttt + 1], float(CLOC),
                                                           None, ALU.is_ge),
                    after=[e_rl] + bf(c4).war())
        bf(c4).written(e_ov)
        e_sl = p.op("v", lambda E, ttt=tt: E.scalar_tensor_tensor(c1[:], sei[:, ttt:ttt + 1],
                                                                  float(CLOC), srl[:, ttt:ttt + 1],
                                                                  ALU.mult, ALU.add),
                    after=[e_ei, e_rl] + bf(c1).war())
        bf(c1).written(e_sl)
        e_sl1b = p.op("v", lambda E: E.tensor_scalar(c2[:], c1[:], -1.0, float(NSLOT),
                                                     ALU.mult, ALU.add),
                      after=[e_sl] + bf(c2).war())
        e_sl1c = p.op("v", lambda E: E.tensor_mul(c2[:], c2[:], c4[:]),
                      after=[e_sl1b, e_ov])
        e_sl2 = p.op("v", lambda E: E.tensor_add(c2[:], c2[:], c1[:]),
                     after=[e_sl1c])
        bf(c1).read(e_sl2)
        bf(c2).written(e_sl2)
        e_si = p.op("v", lambda E, ttt=tt: E.tensor_copy(sslot[:, ttt:ttt + 1], c2[:]),
                    after=[e_sl2])
        bf(c2).read(e_si)
        bf(sslot).read(e_si)
        e_i0 = p.op("v", lambda E, ttt=tt: E.tensor_copy(ci1[:], srl[:, ttt:ttt + 1]),
                    after=[e_rl] + bf(ci1).war())
        bf(ci1).written(e_i0)
        e_i1 = p.op("v", lambda E: E.tensor_scalar(ci2[:], ci1[:], 127, None, ALU.bitwise_and),
                    after=[e_i0] + bf(ci2).war())
        bf(ci2).written(e_i1)
        e_i2 = p.op("v", lambda E: E.tensor_scalar(ci3[:], ci1[:], 7, None, ALU.arith_shift_right),
                    after=[e_i0] + bf(ci3).war())
        bf(ci1).read(e_i2)
        bf(ci3).written(e_i2)
        e_i3 = p.op("v", lambda E: E.tensor_scalar(ci2[:], ci2[:], 97, None, ALU.mult),
                    after=[e_i1])
        bf(ci2).written(e_i3)
        e_i4 = p.op("v", lambda E, ttt=tt: E.tensor_copy(ci4[:], sei[:, ttt:ttt + 1]),
                    after=[e_ei] + bf(ci4).war())
        bf(ci4).written(e_i4)
        e_i5 = p.op("v", lambda E: E.tensor_scalar(ci4[:], ci4[:], 12, None, ALU.mult),
                    after=[e_i4])
        bf(ci4).written(e_i5)
        e_i6 = p.op("v", lambda E: E.tensor_add(ci2[:], ci2[:], ci4[:]), after=[e_i3, e_i5])
        bf(ci4).read(e_i6)
        e_i7 = p.op("v", lambda E: E.tensor_add(ci2[:], ci2[:], ci3[:]), after=[e_i6, e_i2])
        bf(ci3).read(e_i7)
        bf(ci2).written(e_i7)
        e_i8 = p.op("v", lambda E: E.tensor_copy(ci4[:], c4[:]), after=[bf(c4).wr] + bf(ci4).war())
        bf(c4).read(e_i8)
        bf(ci4).written(e_i8)
        e_i9 = p.op("v", lambda E: E.tensor_scalar(ci3[:], ci2[:], -1, 96, ALU.mult, ALU.add),
                    after=[e_i7] + bf(ci3).war())
        bf(ci3).written(e_i9)
        e_i10 = p.op("v", lambda E: E.tensor_mul(ci3[:], ci3[:], ci4[:]), after=[e_i9, e_i8])
        bf(ci4).read(e_i10)
        bf(ci3).written(e_i10)
        e_fi = p.op("v", lambda E: E.tensor_add(fidx[:], ci2[:], ci3[:]),
                    after=[e_i10] + bf(fidx).war())
        bf(ci2).read(e_fi)
        bf(ci3).read(e_fi)
        bf(fidx).written(e_fi)
        e_tk = p.op("v", lambda E, ttt=tt: E.tensor_scalar(c3[:], iotacol[:], float(ttt * 128),
                                                           None, ALU.add),
                    after=[e_ic] + bf(c3).war())
        bf(c3).written(e_tk)
        e_tk2 = p.op("v", lambda E: E.tensor_copy(tokid[:], c3[:]), after=[e_tk] + bf(tokid).war())
        bf(c3).read(e_tk2)
        bf(tokid).written(e_tk2)
        e_sc = p.op("g", lambda E: E.indirect_dma_start(
            out=ttab, out_offset=IndirectOffsetOnAxis(fidx[:], 1),
            in_=tokid[:], in_offset=None), after=[e_fi, e_tk2, e_ti], dma=True)
        bf(fidx).read(e_sc)
        bf(tokid).read(e_sc)
        bf(sslot).read(e_sc)
    bf(sg1c).written(e_g1c)
    bf(sslot).written(e_si)

    e_tl = p.dma("y", tsb[:], ttab, after=p.qfence("g"))
    bf(tsb).written(e_tl)

    for e0 in range(E0):
        e_b1l = p.dma("y", b1s[:], b1_d[e0], after=bf(b1s).war())
        bf(b1s).written(e_b1l)
        e_b2l = p.dma("y", b2s[:], b2_d[e0], after=bf(b2s).war())
        bf(b2s).written(e_b2l)
        for g in range(3):
            for st in range(4):
                lb = st % 2
                col = e0 * 12 + g * 4 + st
                e_ms = None
                e_gi = p.op("v", lambda E, o=gidx[lb], cc=col: E.tensor_copy(o[:], tsb[:, cc:cc + 1]),
                            after=[e_tl] + bf(gidx[lb]).war())
                bf(tsb).read(e_gi)
                bf(gidx[lb]).written(e_gi)
                e_ga = p.op("g", lambda E, o=xgf[lb], ix=gidx[lb]: E.indirect_dma_start(
                    out=o[:], out_offset=None,
                    in_=x1_d, in_offset=IndirectOffsetOnAxis(ix[:], 0)),
                    after=[e_gi] + bf(xgf[lb]).war(), dma=True)
                bf(xgf[lb]).written(e_ga)
                bf(gidx[lb]).read(e_ga)
                e_cb = p.op("v", lambda E, o=xgb[lb], i=xgf[lb]: E.tensor_copy(o[:], i[:]),
                            after=[e_ga] + bf(xgb[lb]).war())
                bf(xgf[lb]).read(e_cb)
                bf(xgb[lb]).written(e_cb)
                for c in range(8):
                    pt = ps_t[c % 2]
                    e_tp = p.op("t", lambda E, o=pt, i=xgb[lb], cc=c:
                                E.transpose(o[:], i[:, cc * 128:(cc + 1) * 128], identb[:]),
                                after=bf(pt).war() + [e_cb, e_id] +
                                      (bf(xgfm).war() if st == 0 and c == 0 else []))
                    bf(xgb[lb]).read(e_tp)
                    bf(pt).written(e_tp)
                    e_ev = p.op("v", lambda E, i=pt, cc=c, ss=st:
                                E.tensor_copy(xgfm[:, cc * 512 + ss * 128: cc * 512 + (ss + 1) * 128], i[:]),
                                after=[e_tp])
                    bf(pt).read(e_ev)
                    bf(xgfm).read(e_ev)
            bf(xgfm).written(e_ev)
            e_xg = bf(xgfm).wr
            for f in range(32):
                wb_ = w1s[f % 2]
                e_wl = p.dma("g", wb_[:].rearrange("p (c f) -> p c f", c=8),
                             w1_d[e0, :, f * 128:(f + 1) * 128].rearrange("(c p) f -> p c f", p=128),
                             after=bf(wb_).war())
                bf(wb_).written(e_wl)
                pse = ps_s[f % 2]
                last = None
                for c in range(8):
                    last = p.op("t", lambda E, o=pse, w=wb_, cc=c:
                                E.matmul(o[:], w[:, cc * 128:(cc + 1) * 128],
                                         xgfm[:, cc * 512:(cc + 1) * 512],
                                         start=(cc == 0), stop=(cc == 7)),
                                after=(bf(pse).war() + [e_wl, e_xg]) if c == 0 else [])
                bf(wb_).read(last)
                bf(xgfm).read(last)
                bf(pse).written(last)
                e_h = p.op("s", lambda E, i=pse, ff=f: E.activation(
                    hfm[:, ff * 512:(ff + 1) * 512], i[:], AF.Relu, bias=b1s[:, ff:ff + 1]),
                    after=[last, e_b1l] + (bf(hfm).war() if f == 0 else []))
                bf(pse).read(e_h)
                bf(hfm).read(e_h)
            bf(hfm).written(e_h)
            e_hd = bf(hfm).wr
            for mchunk in range(8):
                wb_ = w2s[mchunk % 2]
                e_wl = p.dma("g", wb_[:].rearrange("p (f m) -> p f m", f=32),
                             w2_d[e0, :, mchunk * 128:(mchunk + 1) * 128].rearrange("(f p) m -> p f m", p=128),
                             after=bf(wb_).war())
                bf(wb_).written(e_wl)
                pse = ps_s[mchunk % 2]
                last = None
                for f in range(32):
                    last = p.op("t", lambda E, o=pse, w=wb_, ff=f:
                                E.matmul(o[:], w[:, ff * 128:(ff + 1) * 128],
                                         hfm[:, ff * 512:(ff + 1) * 512],
                                         start=(ff == 0), stop=(ff == 31)),
                                after=(bf(pse).war() + [e_wl, e_hd]) if f == 0 else [])
                bf(wb_).read(last)
                bf(hfm).read(last)
                bf(pse).written(last)
                e_y = p.op("v", lambda E, i=pse, mm=mchunk: E.tensor_scalar(
                    yfm[:, mm * 512:(mm + 1) * 512], i[:], b2s[:, mm:mm + 1], None, ALU.add),
                    after=[last, e_b2l] + (bf(yfm).war() if mchunk == 0 else []))
                bf(pse).read(e_y)
                bf(yfm).read(e_y)
            bf(yfm).written(e_y)
            e_yd = bf(yfm).wr
            for st in range(4):
                for mchunk in range(8):
                    pt = ps_t[mchunk % 2]
                    e_tp = p.op("t", lambda E, o=pt, mm=mchunk, ss=st:
                                E.transpose(o[:], yfm[:, mm * 512 + ss * 128: mm * 512 + (ss + 1) * 128],
                                            identb[:]),
                                after=bf(pt).war() + [e_yd] + (bf(ytm).war() if mchunk == 0 else []))
                    bf(yfm).read(e_tp)
                    bf(pt).written(e_tp)
                    e_ev = p.op("v", lambda E, i=pt, mm=mchunk:
                                E.tensor_copy(ytm[:, mm * 128:(mm + 1) * 128], i[:]),
                                after=[e_tp])
                    bf(pt).read(e_ev)
                    bf(ytm).read(e_ev)
                bf(ytm).written(e_ev)
                row0 = e0 * CLOC + g * 512 + st * 128
                e_eo = p.dma("y", eout[row0:row0 + 128, :], ytm[:], after=[bf(ytm).wr])
                bf(ytm).read(e_eo)

    yf_b1 = p.qfence("y")
    for tt in range(NT):
        lb = tt % 2
        e_x1 = p.dma("y", x1t[lb][:], x1_d[tt * 128:(tt + 1) * 128, :], after=bf(x1t[lb]).war())
        bf(x1t[lb]).written(e_x1)
        e_mz = None
        e_mi = p.op("v", lambda E, ttt=tt: E.tensor_copy(midx[:], sslot[:, ttt:ttt + 1]),
                    after=[bf(sslot).wr] + bf(midx).war())
        bf(sslot).read(e_mi)
        bf(midx).written(e_mi)
        e_gm = p.op("g", lambda E: E.indirect_dma_start(
            out=mg[:], out_offset=None,
            in_=eout, in_offset=IndirectOffsetOnAxis(midx[:], 0)),
            after=[e_mi] + bf(mg).war() + yf_b1, dma=True)
        bf(mg).written(e_gm)
        bf(midx).read(e_gm)
        e_y = p.op("v", lambda E, xx=x1t[lb], ttt=tt: E.scalar_tensor_tensor(
            ytile[:], mg[:], sg1c[:, ttt:ttt + 1], xx[:], ALU.mult, ALU.add),
            after=[e_gm, e_x1, bf(sg1c).wr] + bf(ytile).war())
        bf(mg).read(e_y)
        bf(x1t[lb]).read(e_y)
        bf(ytile).written(e_y)
        e_ln = ln_emit(p, bf, ytile, x2t, yscr, ymu, yvar, yrstd, ysum, ysq, epsB, [e_y, e_eps])
        e_st = p.dma("y", x2_d[tt * 128:(tt + 1) * 128, :], x2t[:], after=[e_ln])
        bf(x2t).read(e_st)

    p.op("y", lambda E: E.nop(), after=p.qfence("y") + p.qfence("g"))
    p.finalize()
    es.close()
    return nc


def ln_emit(p, bf, ytile, xout, yscr, ymu, yvar, yrstd, ysum, ysq, eps_t, after):
    e_s1 = p.op("v", lambda E: E.reduce_sum(ysum[:], ytile[:], axis=AX.X),
                after=after + bf(ysum).war())
    bf(ysum).written(e_s1)
    e_s2 = p.op("s", lambda E: E.activation(yscr[:], ytile[:], AF.Square, accum_out=ysq[:]),
                after=after + bf(ysq).war() + bf(yscr).war())
    bf(ysq).written(e_s2)
    bf(yscr).written(e_s2)
    e_mu = p.op("v", lambda E: E.tensor_scalar(ymu[:], ysum[:], 1.0 / D, None, ALU.mult),
                after=[e_s1] + bf(ymu).war())
    bf(ymu).written(e_mu)
    e_m2 = p.op("v", lambda E: E.tensor_scalar(yvar[:], ysq[:], 1.0 / D, None, ALU.mult),
                after=[e_s2] + bf(yvar).war())
    e_mm = p.op("v", lambda E: E.tensor_mul(ysum[:], ymu[:], ymu[:]), after=[e_mu, e_m2])
    e_vv = p.op("v", lambda E: E.tensor_sub(yvar[:], yvar[:], ysum[:]), after=[e_mm])
    bf(ysum).written(e_vv)
    e_l1 = p.op("s", lambda E: E.activation(yrstd[:], yvar[:], AF.Ln, bias=eps_t[:]),
                after=[e_vv] + bf(yrstd).war())
    e_l2 = p.op("s", lambda E: E.activation(yrstd[:], yrstd[:], AF.Exp, scale=-0.5), after=[e_l1])
    bf(yvar).written(e_l2)
    bf(yrstd).written(e_l2)
    e_x = p.op("v", lambda E: E.tensor_scalar(xout[:], ytile[:], ymu[:], yrstd[:],
                                              ALU.subtract, ALU.mult),
               after=[e_l2, e_mu] + bf(xout).war())
    bf(ytile).read(e_x)
    bf(xout).written(e_x)
    return e_x


PHC_STAGE = 5
PHC_Q = True
PHC_RMS = True
PHC_RMS_NM = True
PHC_F_LN = True
PHC_F_SR = True
PHC_F_O = True
PHC_RMS_BC = True
PHC_K = True
PHC_V = True


def build_phaseC():
    nc = bass.Bass("TRN2", target_bir_lowering=False)
    es = ExitStack()
    p = Prog(nc, es)
    SCALE1 = 1.0 / 8.0

    x2_d = nc.dram_tensor("X2", [T, D], F32, kind="ExternalInput").ap()
    wqp_d = nc.dram_tensor("p_WqP", [D, D], F32, kind="ExternalInput").ap()
    wkv_d = nc.dram_tensor("p_Wkv", [D, LAT], F32, kind="ExternalInput").ap()
    wk_d = nc.dram_tensor("p_Wk", [LAT, 512], F32, kind="ExternalInput").ap()
    wv_d = nc.dram_tensor("p_Wv", [LAT, 512], F32, kind="ExternalInput").ap()
    wo_d = nc.dram_tensor("p_Wo_nz", [512, D], F32, kind="ExternalInput").ap()
    qga_d = nc.dram_tensor("p_qgA", [128, 1], F32, kind="ExternalInput").ap()
    kg_d = nc.dram_tensor("p_kg", [128, 1], F32, kind="ExternalInput").ap()
    identb_d = nc.dram_tensor("c_ident_b", [128, 128], BF16, kind="ExternalInput").ap()
    mask_d = nc.dram_tensor("c_mask896", [128, 896], BF16, kind="ExternalInput").ap()
    bd4_d = nc.dram_tensor("c_bd4_b", [128, 4], BF16, kind="ExternalInput").ap()
    bd4T_d = nc.dram_tensor("c_bd4T_b", [4, 128], BF16, kind="ExternalInput").ap()
    bdrep_d = nc.dram_tensor("c_bdrep_b", [128, 128], BF16, kind="ExternalInput").ap()
    onesrowf_d = nc.dram_tensor("c_ones_row_f", [1, 128], F32, kind="ExternalInput").ap()

    x3_d = nc.dram_tensor("X3", [T, D], F32, kind="ExternalOutput").ap()

    sbt = lambda name, shape, dt: es.enter_context(nc.sbuf_tensor(name, shape, dt))
    pst = lambda name, shape, dt: es.enter_context(nc.psum_tensor(name, shape, dt))

    xt = [sbt("xtC0", [128, D], F32)] * 2
    xbf = [sbt("xbfC0", [128, D], BF16)] * 2
    xfmc = [sbt("xfmcC0", [128, 8 * 512], BF16)] * 2
    wqg = [sbt(f"wqg{i}", [128, 8 * 128], BF16) for i in range(2)]
    wkvs = sbt("wkvs", [128, 8 * LAT], BF16)
    wks = sbt("wks", [128, 512], BF16)
    wvs = sbt("wvs", [128, 512], BF16)
    wos = sbt("wos", [128, 4 * D], BF16)
    qga = sbt("qga", [128, 1], F32)
    kgs = sbt("kgs", [128, 1], F32)
    identb = sbt("identbC", [128, 128], BF16)
    maskm = sbt("maskm", [128, 896], BF16)
    bd4 = sbt("bd4", [128, 4], BF16)
    bd4T = sbt("bd4T", [4, 128], BF16)
    bdrep = sbt("bdrep", [128, 128], BF16)
    onesrf = sbt("onesrfC", [1, 128], F32)
    ones_b32 = sbt("ones_b32", [128, 32], BF16)
    kvfm = sbt("kvfm", [128, T], BF16)
    khat = [sbt(f"khat{g}", [128, T], BF16) for g in range(4)]
    qhat = [sbt(f"qhat{g}", [128, T], BF16) for g in range(4)]
    khat3 = sbt("khat_s", [128, T], BF16)
    qhat3 = sbt("qhat_s", [128, T], BF16)
    vtm33 = sbt("vtm33", [128, 32 * 16 * 33], BF16)
    sqt = sbt("sqt", [128, 512], BF16)
    rstdf = sbt("rstdf", [128, 512], F32)
    mxq = sbt("mxq", [128, 1], F32)
    mxk = sbt("mxk", [128, 1], F32)
    mx1 = sbt("mx1", [128, 1], F32)
    mxb = sbt("mxb", [128, 1], BF16)
    epsR = sbt("epsR", [128, 1], F32)
    rawsc = sbt("rawsc", [128, 512], F32)
    s11 = sbt("s11", [1, 4], F32)
    negmh = sbt("negmh", [128, 1], F32)
    pbuf = [sbt(f"pbuf{i}", [128, 512], BF16) for i in range(2)]
    att = sbt("att", [128, 512], F32)
    att3s = sbt("att3s", [128, 512], BF16)
    rlb = sbt("rlb", [128, 512], BF16)
    attfm = [sbt(f"attfm{g}", [128, T], BF16) for g in range(4)]
    atn = sbt("atnC", [128, D], F32)
    ytile = sbt("ytileC", [128, D], F32)
    x3t = ytile
    ysum = sbt("ysumC", [128, 1], F32)
    ysq = sbt("ysqC", [128, 1], F32)
    ymu = sbt("ymuC", [128, 1], F32)
    yvar = sbt("yvarC", [128, 1], F32)
    yrstd = sbt("yrstdC", [128, 1], F32)
    epsC = sbt("epsC", [128, 1], F32)
    yscr = atn

    ps_t = [pst(f"ps_tC{i}", [128, 128], BF16) for i in range(2)]
    ps_s = [pst(f"ps_sC{i}", [128, 512], F32) for i in range(2)]
    ps_q = [pst(f"ps_qC{i}", [128, 512], F32) for i in range(2)]
    ps_n = pst("ps_nC", [4, 512], F32)
    ps_b = pst("ps_bC", [128, 512], F32)
    ps_o = ps_q[0]

    bufs = {}

    def bf(obj):
        key = id(obj)
        if key not in bufs:
            bufs[key] = Buf()
        return bufs[key]

    e_wkv = [p.dma("g", wkvs[:, c * LAT:(c + 1) * LAT], wkv_d[c * 128:(c + 1) * 128, :]) for c in range(8)]
    e_wk = p.dma("g", wks[:], wk_d)
    e_wv = p.dma("g", wvs[:], wv_d)
    e_wo = [p.dma("g", wos[:, g * D:(g + 1) * D], wo_d[g * 128:(g + 1) * 128, :]) for g in range(4)]
    e_qg = p.dma("y", qga[:], qga_d)
    e_kg = p.dma("y", kgs[:], kg_d)
    e_id = p.dma("y", identb[:], identb_d)
    e_mk = p.dma("y", maskm[:], mask_d)
    e_b4 = p.dma("y", bd4[:], bd4_d)
    e_b4T = p.dma("y", bd4T[:], bd4T_d)
    e_bdr = p.dma("y", bdrep[:], bdrep_d)
    e_orf = p.dma("y", onesrf[:], onesrowf_d)
    e_ob = p.op("v", lambda E: E.memset(ones_b32[:], 1.0))
    e_er = p.op("v", lambda E: E.memset(epsR[:], RMSEPS))
    e_mq0 = p.op("v", lambda E: E.memset(mxq[:], 0.0))
    e_mk0 = p.op("v", lambda E: E.memset(mxk[:], 0.0))
    e_ec = p.op("v", lambda E: E.memset(epsC[:], LNEPS))
    e_vo = p.op("v", lambda E: E.memset(
        vtm33[:].rearrange("p (t h x) -> p t h x", t=32, h=16)[:, :, :, 32:33], 1.0))
    bf(vtm33).read(e_vo)

    def rms_full(e_raw, pse, gain_t, gain_ev, mxt, mx0_ev, nscale, dest, dcol, sq_evs):
        if PHC_F_LN:
            e_l0 = p.op("v", lambda E: E.tensor_copy(rstdf[:], ps_b[:]),
                        after=[bf(ps_b).wr] + bf(rstdf).war())
            bf(ps_b).read(e_l0)
            bf(rstdf).written(e_l0)
            e_l1 = p.op("s", lambda E: E.activation(rstdf[:], rstdf[:], AF.Ln,
                                                    bias=epsR[:], scale=nscale),
                        after=[e_l0, e_er])
            e_l2 = p.op("s", lambda E: E.activation(rstdf[:], rstdf[:], AF.Exp, scale=-0.5),
                        after=[e_l1])
            bf(rstdf).written(e_l2)
        else:
            e_l2 = p.op("v", lambda E: E.tensor_copy(rstdf[:], ps_b[:]),
                        after=[bf(ps_b).wr] + bf(rstdf).war())
            bf(ps_b).read(e_l2)
            bf(rstdf).written(e_l2)
        if PHC_F_SR:
            e_sr = p.op("v", lambda E: E.tensor_scalar(rawsc[:], pse[:], gain_t[:], None, ALU.mult),
                        after=[e_raw, gain_ev] + bf(rawsc).war())
        else:
            e_sr = p.op("v", lambda E: E.tensor_copy(rawsc[:], pse[:]),
                        after=[e_raw] + bf(rawsc).war())
        bf(pse).read(e_sr)
        bf(rawsc).written(e_sr)
        if PHC_F_O:
            e_o = p.op("v", lambda E, dd=dcol: E.tensor_mul(dest[:, dd * 512:(dd + 1) * 512],
                                                            rawsc[:], rstdf[:]),
                       after=[e_sr, e_l2])
            bf(rstdf).read(e_o)
        else:
            e_o = p.op("v", lambda E, dd=dcol: E.tensor_copy(dest[:, dd * 512:(dd + 1) * 512],
                                                             rawsc[:]),
                       after=[e_sr, e_l2])
        bf(rawsc).read(e_o)
        bf(dest).read(e_o)
        return e_o

    for tc in range(8):
        xc = xfmc[tc % 2]
        for i4 in range(4):
            tt = tc * 4 + i4
            lb = tt % 2
            e_ld = p.dma("y", xt[lb][:], x2_d[tt * 128:(tt + 1) * 128, :], after=bf(xt[lb]).war())
            bf(xt[lb]).written(e_ld)
            e_cast = p.op("v", lambda E, o=xbf[lb], i=xt[lb]: E.tensor_copy(o[:], i[:]),
                          after=bf(xbf[lb]).war() + [e_ld])
            bf(xt[lb]).read(e_cast)
            bf(xbf[lb]).written(e_cast)
            for c in range(8):
                pt = ps_t[c % 2]
                e_tp = p.op("t", lambda E, o=pt, i=xbf[lb], cc=c:
                            E.transpose(o[:], i[:, cc * 128:(cc + 1) * 128], identb[:]),
                            after=bf(pt).war() + [e_cast, e_id] + (bf(xc).war() if i4 == 0 and c == 0 else []))
                bf(xbf[lb]).read(e_tp)
                bf(pt).written(e_tp)
                e_ev = p.op("v", lambda E, o=xc, i=pt, cc=c, ii=i4:
                            E.tensor_copy(o[:, cc * 512 + ii * 128: cc * 512 + (ii + 1) * 128], i[:]),
                            after=[e_tp])
                bf(pt).read(e_ev)
                bf(xc).read(e_ev)
        bf(xc).written(e_ev)
        e_xfm = bf(xc).wr
        pse = ps_s[0]
        last = None
        for c in range(8):
            last = p.op("t", lambda E, o=pse, cc=c, xcc=xc:
                        E.matmul(o[:], wkvs[:, cc * LAT:(cc + 1) * LAT],
                                 xcc[:, cc * 512:(cc + 1) * 512],
                                 start=(cc == 0), stop=(cc == 7)),
                        after=(bf(pse).war() + [e_xfm] + e_wkv) if c == 0 else [])
        bf(xc).read(last)
        bf(pse).written(last)
        e_kvc = p.op("v", lambda E, i=pse, cc=tc: E.tensor_copy(kvfm[:, cc * 512:(cc + 1) * 512], i[:]),
                     after=[last] + bf(kvfm).raw())
        bf(pse).read(e_kvc)
        bf(kvfm).read(e_kvc)
        for g in (range(4) if PHC_Q else []):
            psA_ = ps_q[0]
            psB_ = ps_q[1]
            wA = wqg[0]
            wB = wqg[1]
            e_wA = p.dma("g", wA[:].rearrange("p (c o) -> p c o", c=8),
                         wqp_d[:, g * 128:(g + 1) * 128].rearrange("(c p) o -> p c o", p=128),
                         after=bf(wA).war())
            bf(wA).written(e_wA)
            e_wB = p.dma("g", wB[:].rearrange("p (c o) -> p c o", c=8),
                         wqp_d[:, 512 + g * 128: 512 + (g + 1) * 128].rearrange("(c p) o -> p c o", p=128),
                         after=bf(wB).war())
            bf(wB).written(e_wB)
            lastA = None
            for c in range(8):
                lastA = p.op("t", lambda E, o=psA_, cc=c, xcc=xc, w=wA:
                             E.matmul(o[:], w[:, cc * 128:(cc + 1) * 128],
                                      xcc[:, cc * 512:(cc + 1) * 512],
                                      start=(cc == 0), stop=(cc == 7)),
                             after=(bf(psA_).war() + [e_xfm, e_wA]) if c == 0 else [])
            bf(xc).read(lastA)
            bf(wA).read(lastA)
            bf(psA_).written(lastA)
            lastB = None
            for c in range(8):
                lastB = p.op("t", lambda E, o=psB_, cc=c, xcc=xc, w=wB:
                             E.matmul(o[:], w[:, cc * 128:(cc + 1) * 128],
                                      xcc[:, cc * 512:(cc + 1) * 512],
                                      start=(cc == 0), stop=(cc == 7)),
                             after=(bf(psB_).war() + [e_xfm, e_wB]) if c == 0 else [])
            bf(xc).read(lastB)
            bf(wB).read(lastB)
            bf(psB_).written(lastB)
            e_sqA = p.op("s", lambda E, i=psA_: E.activation(sqt[:], i[:], AF.Square),
                         after=[lastA] + bf(sqt).war())
            bf(psA_).read(e_sqA)
            bf(sqt).written(e_sqA)
            e_n1 = p.op("t", lambda E: E.matmul(ps_b[:], bdrep[:], sqt[:], start=True, stop=False),
                        after=[e_sqA, e_bdr] + bf(ps_b).war())
            bf(sqt).read(e_n1)
            e_sqB = p.op("s", lambda E, i=psB_: E.activation(sqt[:], i[:], AF.Square),
                         after=[lastB, e_n1] + bf(sqt).war())
            bf(psB_).read(e_sqB)
            bf(sqt).written(e_sqB)
            e_n2e = p.op("t", lambda E: E.matmul(ps_b[:], bdrep[:], sqt[:], start=False, stop=True),
                         after=[e_sqB])
            bf(sqt).read(e_n2e)
            bf(ps_b).written(e_n2e)
            e_qo = rms_full(lastA, psA_, qga, e_qg, mxq, e_mq0, 1.0 / 64.0,
                            qhat[g], tc, None)
        if PHC_Q:
            bf(qhat[0]).read(e_qo)
        for g in (range(4) if PHC_K else []):
            psK = ps_q[0]
            e_k = p.op("t", lambda E, o=psK, gg=g, cc=tc:
                       E.matmul(o[:], wks[:, gg * 128:(gg + 1) * 128],
                                kvfm[:, cc * 512:(cc + 1) * 512],
                                start=True, stop=True),
                       after=bf(psK).war() + [e_kvc, e_wk])
            bf(kvfm).read(e_k)
            bf(psK).written(e_k)
            e_ksq = p.op("s", lambda E, i=psK: E.activation(sqt[:], i[:], AF.Square),
                         after=[e_k] + bf(sqt).war())
            bf(psK).read(e_ksq)
            bf(sqt).written(e_ksq)
            e_kn = p.op("t", lambda E: E.matmul(ps_b[:], bdrep[:], sqt[:], start=True, stop=True),
                        after=[e_ksq, e_bdr] + bf(ps_b).war())
            bf(sqt).read(e_kn)
            bf(ps_b).written(e_kn)
            e_ko = rms_full(e_k, psK, kgs, e_kg, mxk, e_mk0, 1.0 / 32.0,
                            khat[g], tc, None)
        for i4 in (range(4) if PHC_V else []):
            tt = tc * 4 + i4
            psV = ps_q[1]
            e_v = p.op("t", lambda E, o=psV, cc=tc, ii=i4:
                       E.matmul(o[:], kvfm[:, cc * 512 + ii * 128: cc * 512 + (ii + 1) * 128],
                                wvs[:], start=True, stop=True),
                       after=bf(psV).war() + [e_kvc, e_wv])
            bf(kvfm).read(e_v)
            bf(psV).written(e_v)
            e_vo2 = p.op("v", lambda E, i=psV, ttt=tt: E.tensor_copy(
                vtm33[:].rearrange("p (t h x) -> p t h x", t=32, h=16)[:, ttt, :, 0:32],
                i[:].rearrange("p (h x) -> p h x", h=16)),
                after=[e_v, e_vo] + bf(vtm33).raw())
            bf(psV).read(e_vo2)
            bf(vtm33).read(e_vo2)
    bf(kvfm).written(e_kvc)
    if PHC_V:
        bf(vtm33).written(e_vo2)
    for g in range(4):
        if PHC_Q:
            bf(qhat[g]).written(e_qo)
        if PHC_K:
            bf(khat[g]).written(e_ko)

    def gmax(gt, gev, outcol):
        e0 = p.op("v", lambda E, g_=gt: E.tensor_mul(mx1[:], g_[:], g_[:]),
                  after=[gev] + bf(mx1).war())
        bf(gt).read(e0)
        bf(mx1).written(e0)
        e1 = p.op("v", lambda E: E.tensor_copy(mxb[:], mx1[:]),
                  after=[e0] + bf(mxb).war())
        bf(mx1).read(e1)
        bf(mxb).written(e1)
        e2 = p.op("t", lambda E: E.transpose(ps_t[0][0:1, 0:128], mxb[:], identb[:]),
                  after=[e1, e_id] + bf(ps_t[0]).war())
        bf(mxb).read(e2)
        bf(ps_t[0]).written(e2)
        e3 = p.op("v", lambda E, oc=outcol: E.reduce_max(s11[0:1, oc:oc + 1],
                                                         ps_t[0][0:1, 0:128], axis=AX.X),
                  after=[e2] + (bf(s11).war() if outcol == 0 else []))
        bf(ps_t[0]).read(e3)
        bf(s11).read(e3)
        return e3

    e_q4 = gmax(qga, e_qg, 0)
    e_k4 = gmax(kgs, e_kg, 1)
    e_lq = p.op("s", lambda E: E.activation(s11[0:1, 2:3], s11[0:1, 0:1], AF.Ln),
                after=[e_q4])
    e_lk = p.op("s", lambda E: E.activation(s11[0:1, 3:4], s11[0:1, 1:2], AF.Ln),
                after=[e_k4, e_lq])
    e_ad = p.op("v", lambda E: E.tensor_add(s11[0:1, 2:3], s11[0:1, 2:3], s11[0:1, 3:4]),
                after=[e_lk])
    e_ex = p.op("s", lambda E: E.activation(s11[0:1, 2:3], s11[0:1, 2:3], AF.Exp, scale=0.5),
                after=[e_ad])
    e_ng = p.op("v", lambda E: E.tensor_scalar(s11[0:1, 3:4], s11[0:1, 2:3],
                                               -float(np.sqrt(32.0)), None, ALU.mult),
                after=[e_ex])
    bf(s11).written(e_ng)
    e_bc = p.op("t", lambda E: E.matmul(ps_b[0:128, 0:1], onesrf[:], s11[0:1, 3:4],
                                        start=True, stop=True),
                after=[e_ng, e_orf] + bf(ps_b).war())
    bf(s11).read(e_bc)
    bf(ps_b).written(e_bc)
    e_nm = p.op("v", lambda E: E.tensor_copy(negmh[:], ps_b[0:128, 0:1]),
                after=[e_bc] + bf(negmh).war())
    bf(ps_b).read(e_nm)
    bf(negmh).written(e_nm)

    for h in (range(16) if PHC_STAGE >= 4 else []):
        g = h // 4
        hb = 32 * (h % 4)
        if h % 4 == 3:
            e_ck = p.dma("y", khat3[0:32, :], khat[g][96:128, :],
                         after=bf(khat3).war() + [bf(khat[g]).wr])
            bf(khat[g]).read(e_ck)
            bf(khat3).written(e_ck)
            e_cq = p.dma("y", qhat3[0:32, :], qhat[g][96:128, :],
                         after=bf(qhat3).war() + [bf(qhat[g]).wr])
            bf(qhat[g]).read(e_cq)
            bf(qhat3).written(e_cq)
            ksrc, kb0, e_kh = khat3, 0, e_ck
            qsrc, qb0, e_qh = qhat3, 0, e_cq
        else:
            ksrc, kb0, e_kh = khat[g], hb, bf(khat[g]).wr
            qsrc, qb0, e_qh = qhat[g], hb, bf(qhat[g]).wr
        for qc in range(8):
            nkt = 4 * qc + 4
            last_o = None
            for kt in range(nkt):
                pse = ps_s[kt % 2]
                e_s = p.op("t", lambda E, o=pse, kk=kt, qq=qc, ks=ksrc, kb=kb0, qs=qsrc, qb=qb0:
                           E.matmul(o[:], ks[kb:kb + 32, kk * 128:(kk + 1) * 128],
                                    qs[qb:qb + 32, qq * 512:(qq + 1) * 512],
                                    start=True, stop=True),
                           after=bf(pse).war() + [e_kh, e_qh])
                bf(ksrc).read(e_s)
                bf(qsrc).read(e_s)
                bf(pse).written(e_s)
                rel = kt * 128 - qc * 512
                if rel >= 0:
                    e_s = p.op("v", lambda E, o=pse, rr=rel: E.tensor_add(
                        o[:], o[:], maskm[:, 384 - rr: 384 - rr + 512]),
                        after=[e_s, e_mk])
                    bf(pse).written(e_s)
                pb = pbuf[kt % 2]
                e_p = p.op("s", lambda E, i=pse, o=pb: E.activation(
                    o[:], i[:], AF.Exp, bias=negmh[:], scale=SCALE1),
                    after=[e_s, e_nm] + bf(pb).war())
                bf(pse).read(e_p)
                bf(pb).written(e_p)
                e_o = p.op("t", lambda E, o=ps_o, pbb=pb, kk=kt, hh=h, nn=nkt:
                           E.matmul(o[0:33, :],
                                    vtm33[:, (kk * 16 + hh) * 33:(kk * 16 + hh) * 33 + 33],
                                    pbb[:], start=(kk == 0), stop=(kk == nn - 1)),
                           after=[e_p, bf(vtm33).wr] + (bf(ps_o).war() if kt == 0 else []))
                bf(pb).read(e_o)
                bf(vtm33).read(e_o)
                last_o = e_o
            bf(ps_o).written(last_o)
            e_ev = p.op("v", lambda E: E.tensor_copy(att[0:33, :], ps_o[0:33, :]),
                        after=[last_o] + bf(att).war())
            bf(ps_o).read(e_ev)
            bf(att).written(e_ev)
            e_rc0 = p.op("v", lambda E: E.reciprocal(att[32:33, :], att[32:33, :]),
                         after=[e_ev])
            bf(att).written(e_rc0)
            e_rc = p.op("v", lambda E: E.tensor_copy(rlb[32:33, :], att[32:33, :]),
                        after=[e_rc0] + bf(rlb).war())
            bf(att).read(e_rc)
            bf(rlb).written(e_rc)
            e_b = p.op("t", lambda E: E.matmul(ps_b[0:32, :], ones_b32[32:33, :],
                                               rlb[32:33, :], start=True, stop=True),
                       after=[e_rc, e_ob] + bf(ps_b).war())
            bf(rlb).read(e_b)
            bf(ps_b).written(e_b)
            if h % 4 == 3:
                e_an = p.op("v", lambda E: E.tensor_mul(att3s[0:32, :], att[0:32, :],
                                                        ps_b[0:32, :]),
                            after=[e_b] + bf(att3s).war())
                bf(att).read(e_an)
                bf(ps_b).read(e_an)
                bf(att3s).written(e_an)
                e_wr = p.dma("y", attfm[g][96:128, qc * 512:(qc + 1) * 512], att3s[0:32, :],
                             after=[e_an])
                bf(att3s).read(e_wr)
                bf(attfm[g]).read(e_wr)
            else:
                e_an = p.op("v", lambda E, bb=hb, gg=g, qq=qc: E.tensor_mul(
                    attfm[gg][bb:bb + 32, qq * 512:(qq + 1) * 512], att[0:32, :], ps_b[0:32, :]),
                    after=[e_b] + bf(attfm[g]).raw())
                bf(att).read(e_an)
                bf(ps_b).read(e_an)
                bf(attfm[g]).read(e_an)
    vfence_t = sbt("vfence_t", [1, 1], F32)
    e_vfence = p.op("v", lambda E: E.memset(vfence_t[:], 0.0))
    for g in range(4):
        bf(attfm[g]).written(e_vfence)
    yf_att = p.qfence("y")

    if PHC_STAGE >= 4.5:
        pass
    else:
        srcdbg = khat[0] if PHC_K else (qhat[0] if PHC_Q else kvfm)
        for ddt in range(4):
            e_dbg = p.op("v", lambda E, dd=ddt, sd=srcdbg: E.tensor_copy(
                atn[:, dd * 128:(dd + 1) * 128], sd[:, dd * 128:(dd + 1) * 128]),
                after=[bf(srcdbg).wr, e_vfence] + (bf(atn).war() if ddt == 0 else []))
            bf(atn).read(e_dbg)
        bf(atn).written(e_dbg)
        e_dst = p.dma("y", x3_d[0:128, 0:512], atn[:, 0:512], after=[bf(atn).wr])
        p.op("y", lambda E: E.nop(), after=p.qfence("y") + p.qfence("g"))
        p.finalize()
        es.close()
        return nc

    for tt in range(NT):
        lb = tt % 2
        e_ld = p.dma("y", xt[lb][:], x2_d[tt * 128:(tt + 1) * 128, :], after=bf(xt[lb]).war())
        bf(xt[lb]).written(e_ld)
        for half in range(2):
            pse = ps_s[half]
            last = None
            for g in range(4):
                last = p.op("t", lambda E, o=pse, gg=g, hh=half, ttt=tt:
                            E.matmul(o[:], attfm[gg][:, ttt * 128:(ttt + 1) * 128],
                                     wos[:, gg * D + hh * 512: gg * D + (hh + 1) * 512],
                                     start=(gg == 0), stop=(gg == 3)),
                            after=(bf(pse).war() + [bf(attfm[g]).wr] + e_wo + yf_att)
                                  if g == 0 else [bf(attfm[g]).wr])
                bf(attfm[g]).read(last)
            bf(pse).written(last)
            e_at = p.op("v", lambda E, i=pse, hh=half: E.tensor_copy(
                atn[:, hh * 512:(hh + 1) * 512], i[:]),
                after=[last] + (bf(atn).war() if half == 0 else []))
            bf(pse).read(e_at)
            bf(atn).read(e_at)
        bf(atn).written(e_at)
        e_y = p.op("v", lambda E, xx=xt[lb]: E.tensor_add(ytile[:], xx[:], atn[:]),
                   after=[e_ld, bf(atn).wr] + bf(ytile).war())
        bf(xt[lb]).read(e_y)
        bf(atn).read(e_y)
        bf(ytile).written(e_y)
        e_ln = ln_emit(p, bf, ytile, x3t, yscr, ymu, yvar, yrstd, ysum, ysq, epsC, [e_y, e_ec])
        e_st = p.dma("y", x3_d[tt * 128:(tt + 1) * 128, :], x3t[:], after=[e_ln])
        bf(x3t).read(e_st)

    p.op("y", lambda E: E.nop(), after=p.qfence("y") + p.qfence("g"))
    p.finalize()
    es.close()
    return nc


def build_phaseD():
    nc = bass.Bass("TRN2", target_bir_lowering=False)
    es = ExitStack()
    p = Prog(nc, es)
    BIG = 30000.0

    x3_d = nc.dram_tensor("X3", [T, D], F32, kind="ExternalInput").ap()
    gw_d = nc.dram_tensor("p_gate_wT", [D, E1], F32, kind="ExternalInput").ap()
    eb_d = nc.dram_tensor("p_ebias_row", [1, E1], F32, kind="ExternalInput").ap()
    w1_d = nc.dram_tensor("p_w1T_1", [E1, D, D], F32, kind="ExternalInput").ap()
    w3_d = nc.dram_tensor("p_w3T_1", [E1, D, D], F32, kind="ExternalInput").ap()
    w2_d = nc.dram_tensor("p_w2T_1", [E1, D, D], F32, kind="ExternalInput").ap()
    sw1_d = nc.dram_tensor("p_sw1T", [D, D], F32, kind="ExternalInput").ap()
    sw3_d = nc.dram_tensor("p_sw3T", [D, D], F32, kind="ExternalInput").ap()
    sw2_d = nc.dram_tensor("p_sw2T", [D, D], F32, kind="ExternalInput").ap()
    identb_d = nc.dram_tensor("c_ident_b", [128, 128], BF16, kind="ExternalInput").ap()
    identf_d = nc.dram_tensor("c_ident_f", [128, 128], F32, kind="ExternalInput").ap()
    sel16_d = nc.dram_tensor("c_sel16_b", [16, 16 * 128], BF16, kind="ExternalInput").ap()
    onesrowf_d = nc.dram_tensor("c_ones_row_f", [1, 128], F32, kind="ExternalInput").ap()
    onescolf_d = nc.dram_tensor("c_ones_col_f", [128, 1], F32, kind="ExternalInput").ap()
    onescolb_d = nc.dram_tensor("c_ones_col_b", [128, 1], BF16, kind="ExternalInput").ap()

    out_d = nc.dram_tensor("OUT", [T, D], F32, kind="ExternalOutput").ap()
    acc_hbm = nc.dram_tensor("ACC", [D, T], F32, kind="Internal").ap()

    sbt = lambda name, shape, dt: es.enter_context(nc.sbuf_tensor(name, shape, dt))
    pst = lambda name, shape, dt: es.enter_context(nc.psum_tensor(name, shape, dt))

    xt = [sbt(f"xtD{i}", [128, D], F32) for i in range(2)]
    xbf = sbt("xbfD", [128, D], BF16)
    x3fm = sbt("x3fm", [128, 8 * T], BF16)
    gws = sbt("gws", [128, 8 * E1], BF16)
    ebb = sbt("ebb", [128, E1], F32)
    identb = sbt("identbD", [128, 128], BF16)
    identf = sbt("identfD", [128, 128], F32)
    sel16 = sbt("sel16", [16, 16 * 128], BF16)
    onesrf = sbt("onesrfD", [1, 128], F32)
    onescf = sbt("onescfD", [128, 1], F32)
    onescb = sbt("onescbD", [128, 1], BF16)
    lg = sbt("lgD", [128, E1], F32)
    bsd = sbt("bsd", [128, E1], F32)
    m1 = sbt("m1D", [128, 1], F32)
    t16 = sbt("t16D", [128, E1], F32)
    sel = sbt("selD", [128, E1], F32)
    rm = sbt("rmD", [128, E1], F32)
    nmx = sbt("nmxD", [128, 1], F32)
    pex = sbt("pexD", [128, E1], F32)
    rs = sbt("rsD", [128, 1], F32)
    sgt = sbt("sgt", [128, NT * E1], F32)
    sgtb = sbt("sgtb", [128, E1], BF16)
    gfm = sbt("gfm", [16, T], BF16)
    w1s = sbt("w1sD", [128, 8 * D], BF16)
    w3s = sbt("w3sD", [128, 8 * D], BF16)
    w2s = sbt("w2sD", [128, 8 * D], BF16)
    asl = sbt("asl", [128, 512], BF16)
    gbs = sbt("gbs", [128, 512], BF16)
    hfm = sbt("hfmD", [128, 8 * 512], BF16)
    yst = sbt("ystD", [128, 512], F32)
    accs = sbt("accs", [128, 8 * 512], F32)
    youtf = sbt("youtf", [128, 8 * 512], F32)
    mrow = sbt("mrow", [1, 512], F32)
    vrow = sbt("vrow", [1, 512], F32)
    rrow = sbt("rrow", [1, 512], F32)
    epsD = sbt("epsD", [1, 1], F32)
    sqrow = sbt("sqrow", [128, 512], BF16)
    otm = sbt("otmD", [128, D], F32)

    ps_t = [pst(f"ps_tD{i}", [128, 128], BF16) for i in range(2)]
    ps_tf = [pst(f"ps_tfD{i}", [128, 128], F32) for i in range(2)]
    ps_s = [pst(f"ps_sD{i}", [128, 512], F32) for i in range(2)]
    ps_g = pst("ps_gD", [128, 512], F32)
    ps_r = pst("ps_rD", [1, 512], F32)

    bufs = {}

    def bf(obj):
        key = id(obj)
        if key not in bufs:
            bufs[key] = Buf()
        return bufs[key]

    e_gw = p.dma("g", gws[:].rearrange("p (c e) -> p c e", c=8),
                 gw_d.rearrange("(c p) e -> p c e", p=128))
    e_eb = p.dma("y", ebb[0:1, :], eb_d)
    e_id = p.dma("y", identb[:], identb_d)
    e_idf = p.dma("y", identf[:], identf_d)
    e_sl = p.dma("y", sel16[:], sel16_d)
    e_orf = p.dma("y", onesrf[:], onesrowf_d)
    e_ocf = p.dma("y", onescf[:], onescolf_d)
    e_ocb = p.dma("y", onescb[:], onescolb_d)
    e_ed = p.op("v", lambda E: E.memset(epsD[:], LNEPS))
    e_eb2 = p.op("t", lambda E: E.matmul(ps_g[0:128, 0:E1], onesrf[:], ebb[0:1, :],
                                         start=True, stop=True),
                 after=[e_eb, e_orf] + bf(ps_g).war())
    bf(ps_g).written(e_eb2)
    e_eb3 = p.op("v", lambda E: E.tensor_copy(ebb[:], ps_g[0:128, 0:E1]), after=[e_eb2])
    bf(ps_g).read(e_eb3)
    bf(ebb).written(e_eb3)

    for tt in range(NT):
        lb = tt % 2
        e_ld = p.dma("y", xt[lb][:], x3_d[tt * 128:(tt + 1) * 128, :], after=bf(xt[lb]).war())
        bf(xt[lb]).written(e_ld)
        e_cast = p.op("v", lambda E, i=xt[lb]: E.tensor_copy(xbf[:], i[:]),
                      after=bf(xbf).war() + [e_ld])
        bf(xt[lb]).read(e_cast)
        bf(xbf).written(e_cast)
        for c in range(8):
            pt = ps_t[c % 2]
            e_tp = p.op("t", lambda E, o=pt, cc=c:
                        E.transpose(o[:], xbf[:, cc * 128:(cc + 1) * 128], identb[:]),
                        after=bf(pt).war() + [e_cast, e_id])
            bf(xbf).read(e_tp)
            bf(pt).written(e_tp)
            e_ev = p.op("v", lambda E, i=pt, cc=c, ttt=tt:
                        E.tensor_copy(x3fm[:, cc * T + ttt * 128: cc * T + (ttt + 1) * 128], i[:]),
                        after=[e_tp])
            bf(pt).read(e_ev)
            bf(x3fm).read(e_ev)
        psr = ps_g
        last = None
        for c in range(8):
            last = p.op("t", lambda E, o=psr, cc=c, ttt=tt:
                        E.matmul(o[0:128, 0:E1],
                                 x3fm[:, cc * T + ttt * 128: cc * T + (ttt + 1) * 128],
                                 gws[:, cc * E1:(cc + 1) * E1],
                                 start=(cc == 0), stop=(cc == 7)),
                        after=(bf(psr).war() + [e_ev, e_gw]) if c == 0 else [])
        bf(x3fm).read(last)
        bf(psr).written(last)
        e_lg = p.op("v", lambda E: E.tensor_copy(lg[:], psr[0:128, 0:E1]),
                    after=[last] + bf(lg).war())
        bf(psr).read(e_lg)
        bf(lg).written(e_lg)
        e_b = p.op("v", lambda E: E.tensor_add(bsd[:], lg[:], ebb[:]),
                   after=[e_lg, bf(ebb).wr] + bf(bsd).war())
        bf(bsd).written(e_b)
        e_prev = e_b
        for r in range(4):
            e_m = p.op("v", lambda E: E.reduce_max(m1[:], bsd[:], axis=AX.X),
                       after=[e_prev] + bf(m1).war())
            bf(m1).written(e_m)
            if r < 3:
                e_k = p.op("v", lambda E: E.tensor_scalar(t16[:], bsd[:], m1[:], None, ALU.is_ge),
                           after=[e_m] + bf(t16).war())
                e_k2 = p.op("v", lambda E: E.tensor_scalar(t16[:], t16[:], -BIG, None, ALU.mult),
                            after=[e_k])
                e_prev = p.op("v", lambda E: E.tensor_add(bsd[:], bsd[:], t16[:]),
                              after=[e_k2])
                bf(t16).written(e_prev)
                bf(bsd).written(e_prev)
        e_b2 = p.op("v", lambda E: E.tensor_add(bsd[:], lg[:], ebb[:]),
                    after=[e_m] + bf(bsd).war())
        bf(bsd).written(e_b2)
        e_se = p.op("v", lambda E: E.tensor_scalar(sel[:], bsd[:], m1[:], None, ALU.is_ge),
                    after=[e_b2, e_m] + bf(sel).war())
        bf(bsd).read(e_se)
        bf(m1).read(e_se)
        bf(sel).written(e_se)
        e_t = p.op("v", lambda E: E.tensor_scalar(t16[:], sel[:], BIG, -BIG, ALU.mult, ALU.add),
                   after=[e_se] + bf(t16).war())
        bf(t16).written(e_t)
        e_rm = p.op("v", lambda E: E.tensor_add(rm[:], lg[:], t16[:]),
                    after=[e_t, e_lg] + bf(rm).war())
        bf(lg).read(e_rm)
        bf(t16).read(e_rm)
        bf(rm).written(e_rm)
        e_mx = p.op("v", lambda E: E.reduce_max(nmx[:], rm[:], axis=AX.X, negate=True),
                    after=[e_rm] + bf(nmx).war())
        bf(nmx).written(e_mx)
        e_px = p.op("s", lambda E: E.activation(pex[:], rm[:], AF.Exp, bias=nmx[:],
                                                accum_out=rs[:]),
                    after=[e_mx] + bf(pex).war() + bf(rs).war())
        bf(rm).read(e_px)
        bf(pex).written(e_px)
        bf(rs).written(e_px)
        e_rc = p.op("v", lambda E: E.reciprocal(rs[:], rs[:]), after=[e_px])
        bf(rs).written(e_rc)
        e_g = p.op("v", lambda E, ttt=tt: E.tensor_scalar(sgt[:, ttt * E1:(ttt + 1) * E1],
                                                          pex[:], rs[:], None, ALU.mult),
                   after=[e_rc])
        bf(pex).read(e_g)
        bf(rs).read(e_g)
        bf(sgt).read(e_g)
        e_gb = p.op("v", lambda E, ttt=tt: E.tensor_copy(sgtb[:], sgt[:, ttt * E1:(ttt + 1) * E1]),
                    after=[e_g] + bf(sgtb).war())
        bf(sgtb).written(e_gb)
        pt = ps_t[0]
        e_tg = p.op("t", lambda E, o=pt: E.transpose(o[0:16, 0:128], sgtb[:, 0:16][0:128, 0:16],
                                                     identb[:]),
                    after=[e_gb, e_id] + bf(pt).war())
        bf(sgtb).read(e_tg)
        bf(pt).written(e_tg)
        e_gf = p.op("v", lambda E, i=pt, ttt=tt: E.tensor_copy(
            gfm[0:16, ttt * 128:(ttt + 1) * 128], i[0:16, 0:128]),
            after=[e_tg])
        bf(pt).read(e_gf)
        bf(gfm).read(e_gf)
    bf(x3fm).written(e_ev)
    bf(gfm).written(e_gf)
    bf(sgt).written(e_g)
    e_x3fm = bf(x3fm).wr

    order = [16] + list(range(16))
    for e1x in order:
        shared = (e1x == 16)
        if shared:
            srcs = (sw1_d, sw3_d, sw2_d)
        else:
            srcs = (w1_d[e1x], w3_d[e1x], w2_d[e1x])
        e_w1 = p.dma("g", w1s[:].rearrange("p (c i) -> p c i", c=8),
                     srcs[0].rearrange("(c p) i -> p c i", p=128), after=bf(w1s).war())
        bf(w1s).written(e_w1)
        e_w3 = p.dma("g", w3s[:].rearrange("p (c i) -> p c i", c=8),
                     srcs[1].rearrange("(c p) i -> p c i", p=128), after=bf(w3s).war())
        bf(w3s).written(e_w3)
        e_w2 = p.dma("g", w2s[:].rearrange("p (c m) -> p c m", c=8),
                     srcs[2].rearrange("(c p) m -> p c m", p=128), after=bf(w2s).war())
        bf(w2s).written(e_w2)
        for tch in range(8):
            if not shared:
                e_gbx = p.op("t", lambda E, ee=e1x, cc=tch: E.matmul(
                    ps_g[:], sel16[:, ee * 128:(ee + 1) * 128],
                    gfm[0:16, cc * 512:(cc + 1) * 512], start=True, stop=True),
                    after=bf(ps_g).war() + [e_sl, bf(gfm).wr])
                bf(gfm).read(e_gbx)
                bf(ps_g).written(e_gbx)
                e_gbe = p.op("v", lambda E: E.tensor_copy(gbs[:], ps_g[:]),
                             after=[e_gbx] + bf(gbs).war())
                bf(ps_g).read(e_gbe)
                bf(gbs).written(e_gbe)
            for ic in range(8):
                psa = ps_s[0]
                psu = ps_s[1]
                lastA = None
                for c in range(8):
                    lastA = p.op("t", lambda E, o=psa, cc=c, ii=ic, cch=tch:
                                 E.matmul(o[:], w1s[:, cc * D + ii * 128: cc * D + (ii + 1) * 128],
                                          x3fm[:, cc * T + cch * 512: cc * T + (cch + 1) * 512],
                                          start=(cc == 0), stop=(cc == 7)),
                                 after=(bf(psa).war() + [e_w1, e_x3fm]) if c == 0 else [])
                bf(w1s).read(lastA)
                bf(x3fm).read(lastA)
                bf(psa).written(lastA)
                e_si = p.op("s", lambda E, i=psa: E.activation(asl[:], i[:], AF.Silu),
                            after=[lastA] + bf(asl).war())
                bf(psa).read(e_si)
                bf(asl).written(e_si)
                lastU = None
                for c in range(8):
                    lastU = p.op("t", lambda E, o=psu, cc=c, ii=ic, cch=tch:
                                 E.matmul(o[:], w3s[:, cc * D + ii * 128: cc * D + (ii + 1) * 128],
                                          x3fm[:, cc * T + cch * 512: cc * T + (cch + 1) * 512],
                                          start=(cc == 0), stop=(cc == 7)),
                                 after=(bf(psu).war() + [e_w3, e_x3fm]) if c == 0 else [])
                bf(w3s).read(lastU)
                bf(x3fm).read(lastU)
                bf(psu).written(lastU)
                e_h = p.op("v", lambda E, i=psu, ii=ic: E.tensor_mul(
                    hfm[:, ii * 512:(ii + 1) * 512], asl[:], i[:]),
                    after=[e_si, lastU] + (bf(hfm).war() if ic == 0 else []))
                bf(asl).read(e_h)
                bf(psu).read(e_h)
                bf(hfm).read(e_h)
                if not shared:
                    e_h = p.op("v", lambda E, ii=ic: E.tensor_mul(
                        hfm[:, ii * 512:(ii + 1) * 512], hfm[:, ii * 512:(ii + 1) * 512],
                        gbs[:]),
                        after=[e_h, bf(gbs).wr])
                    bf(gbs).read(e_h)
                    bf(hfm).read(e_h)
            bf(hfm).written(e_h)
            e_hd = bf(hfm).wr
            for mc in range(8):
                psy = ps_s[mc % 2]
                lastY = None
                for ic in range(8):
                    lastY = p.op("t", lambda E, o=psy, mm=mc, ii=ic:
                                 E.matmul(o[:], w2s[:, ii * D + mm * 128: ii * D + (mm + 1) * 128],
                                          hfm[:, ii * 512:(ii + 1) * 512],
                                          start=(ii == 0), stop=(ii == 7)),
                                 after=(bf(psy).war() + [e_w2, e_hd]) if ic == 0 else [])
                bf(w2s).read(lastY)
                bf(hfm).read(lastY)
                bf(psy).written(lastY)
                e_ye = p.op("v", lambda E, i=psy: E.tensor_copy(yst[:], i[:]),
                            after=[lastY] + bf(yst).war())
                bf(psy).read(e_ye)
                bf(yst).written(e_ye)
                if shared:
                    e_acc = p.dma("g", acc_hbm[mc * 128:(mc + 1) * 128, tch * 512:(tch + 1) * 512],
                                  yst[:], after=[e_ye])
                else:
                    e_acc = p.op("g", lambda E, mm=mc, cch=tch: E.dma_start(
                        out=acc_hbm[mm * 128:(mm + 1) * 128, cch * 512:(cch + 1) * 512],
                        in_=yst[:], accum_op=ALU.add),
                        after=[e_ye], dma=True)
                bf(yst).read(e_acc)

    gf_all = p.qfence("g")
    for tch in range(8):
        e_lds = []
        for mc in range(8):
            e_l = p.dma("y", accs[:, mc * 512:(mc + 1) * 512],
                        acc_hbm[mc * 128:(mc + 1) * 128, tch * 512:(tch + 1) * 512],
                        after=(bf(accs).war() if mc == 0 else []) + gf_all)
            bf(accs).read(e_l)
            e_lds.append(e_l)
        bf(accs).written(e_lds[-1])
        e_y = None
        for mc in range(8):
            e_y = p.op("v", lambda E, mm=mc, cch=tch: E.tensor_add(
                youtf[:, mm * 512:(mm + 1) * 512], accs[:, mm * 512:(mm + 1) * 512],
                x3fm[:, mm * T + cch * 512: mm * T + (cch + 1) * 512]),
                after=[e_lds[mc], e_x3fm] + (bf(youtf).war() if mc == 0 else []))
            bf(accs).read(e_y)
            bf(x3fm).read(e_y)
            bf(youtf).read(e_y)
        bf(youtf).written(e_y)
        e_yd = bf(youtf).wr
        lastS = None
        for mc in range(8):
            lastS = p.op("t", lambda E, mm=mc: E.matmul(
                ps_r[0:1, :], onescf[:], youtf[:, mm * 512:(mm + 1) * 512],
                start=(mm == 0), stop=(mm == 7)),
                after=(bf(ps_r).war() + [e_yd, e_ocf]) if mc == 0 else [])
            bf(youtf).read(lastS)
        bf(ps_r).written(lastS)
        e_mr = p.op("v", lambda E: E.tensor_scalar(mrow[:], ps_r[0:1, :], 1.0 / D, None, ALU.mult),
                    after=[lastS] + bf(mrow).war())
        bf(ps_r).read(e_mr)
        bf(mrow).written(e_mr)
        lastQ = None
        for mc in range(8):
            e_sq = p.op("s", lambda E, mm=mc: E.activation(sqrow[:], youtf[:, mm * 512:(mm + 1) * 512],
                                                           AF.Square),
                        after=[e_yd] + bf(sqrow).war())
            bf(youtf).read(e_sq)
            bf(sqrow).written(e_sq)
            lastQ = p.op("t", lambda E, mm=mc: E.matmul(
                ps_r[0:1, :], onescb[:], sqrow[:],
                start=(mm == 0), stop=(mm == 7)),
                after=[e_sq, e_ocb] + (bf(ps_r).war() if mc == 0 else []))
            bf(sqrow).read(lastQ)
        bf(ps_r).written(lastQ)
        e_vr = p.op("v", lambda E: E.tensor_scalar(vrow[:], ps_r[0:1, :], 1.0 / D, None, ALU.mult),
                    after=[lastQ] + bf(vrow).war())
        bf(ps_r).read(e_vr)
        e_m2 = p.op("v", lambda E: E.tensor_mul(rrow[:], mrow[:], mrow[:]),
                    after=[e_mr, e_vr] + bf(rrow).war())
        e_v2 = p.op("v", lambda E: E.tensor_sub(vrow[:], vrow[:], rrow[:]), after=[e_m2])
        bf(vrow).written(e_v2)
        e_l1 = p.op("s", lambda E: E.activation(rrow[:], vrow[:], AF.Ln, bias=epsD[:]),
                    after=[e_v2, e_ed] + bf(rrow).war())
        e_l2 = p.op("s", lambda E: E.activation(rrow[:], rrow[:], AF.Exp, scale=-0.5), after=[e_l1])
        bf(rrow).written(e_l2)
        e_bm = p.op("t", lambda E: E.matmul(ps_s[0][:], onesrf[:], mrow[:], start=True, stop=True),
                    after=[e_mr, e_orf] + bf(ps_s[0]).war())
        bf(mrow).read(e_bm)
        bf(ps_s[0]).written(e_bm)
        e_br = p.op("t", lambda E: E.matmul(ps_s[1][:], onesrf[:], rrow[:], start=True, stop=True),
                    after=[e_l2, e_orf] + bf(ps_s[1]).war())
        bf(rrow).read(e_br)
        bf(ps_s[1]).written(e_br)
        for mc in range(8):
            e_a1 = p.op("v", lambda E, mm=mc: E.tensor_sub(
                youtf[:, mm * 512:(mm + 1) * 512], youtf[:, mm * 512:(mm + 1) * 512], ps_s[0][:]),
                after=[e_bm, e_yd])
            bf(ps_s[0]).read(e_a1)
            e_a2 = p.op("v", lambda E, mm=mc: E.tensor_mul(
                youtf[:, mm * 512:(mm + 1) * 512], youtf[:, mm * 512:(mm + 1) * 512], ps_s[1][:]),
                after=[e_a1, e_br])
            bf(ps_s[1]).read(e_a2)
            bf(youtf).written(e_a2)
        e_done = bf(youtf).wr
        for i4 in range(4):
            tt = tch * 4 + i4
            for mc in range(8):
                ptf = ps_tf[mc % 2]
                e_tp = p.op("t", lambda E, o=ptf, mm=mc, ii=i4: E.transpose(
                    o[:], youtf[:, mm * 512 + ii * 128: mm * 512 + (ii + 1) * 128], identf[:]),
                    after=bf(ptf).war() + [e_done, e_idf])
                bf(youtf).read(e_tp)
                bf(ptf).written(e_tp)
                e_ot = p.op("v", lambda E, i=ptf, mm=mc: E.tensor_copy(
                    otm[:, mm * 128:(mm + 1) * 128], i[:]),
                    after=[e_tp] + (bf(otm).war() if mc == 0 else []))
                bf(ptf).read(e_ot)
                bf(otm).read(e_ot)
            bf(otm).written(e_ot)
            e_st = p.dma("y", out_d[tt * 128:(tt + 1) * 128, :], otm[:], after=[bf(otm).wr])
            bf(otm).read(e_st)

    p.op("y", lambda E: E.memset(epsD[:], LNEPS) if False else E.nop(),
         after=p.qfence("y") + p.qfence("g"))
    p.finalize()
    es.close()
    return nc


def kernel(**inputs):
    prep = prep_inputs(inputs)
    consts = make_consts()
    x = inputs["L0_attn_x"]

    ncA = build_phaseA()
    mapsA = []
    for b in range(B):
        m = {"xb": np.ascontiguousarray(x[b])}
        for k in ("p_qkv_wT", "p_qkv_b", "p_qkv_b_row", "p_proj_wT", "p_proj_b_row", "p_wg"):
            m[k] = prep[k]
        for k in ("c_ident_b", "c_triu_b", "c_ones_row_b", "c_ones_col_b", "c_ones128_b", "c_iota8"):
            m[k] = consts[k]
        mapsA.append(m)
    resA = run_bass_kernel_spmd(ncA, mapsA, core_ids=list(range(B)))
    X1 = [resA.results[b]["X1"] for b in range(B)]
    CALL = np.ascontiguousarray(
        np.stack([resA.results[b]["COUNTS"][0] for b in range(B)]).astype(np.float32))

    ncB = build_phaseB()
    zrow = np.zeros((1, D), np.float32)
    mapsB = []
    for b in range(B):
        pm = np.zeros((B, 1), np.float32)
        pm[:b] = 1.0
        m = {"X1": np.ascontiguousarray(np.concatenate([X1[b], zrow], 0)),
             "G1V": resA.results[b]["G1V"], "EIX": resA.results[b]["EIX"],
             "RLOC": resA.results[b]["RLOC"], "CALL": CALL, "PMASK": pm,
             "p_w1_0": prep["p_w1_0"], "p_b1_0": prep["p_b1_0"],
             "p_w2_0": prep["p_w2_0"], "p_b2_0": prep["p_b2_0"]}
        for k in ("c_ident_b", "c_iota8", "c_iotacol", "c_ones_row_f"):
            m[k] = consts[k]
        mapsB.append(m)
    resB = run_bass_kernel_spmd(ncB, mapsB, core_ids=list(range(B)))

    ncC = build_phaseC()
    mapsC = []
    for b in range(B):
        m = {"X2": resB.results[b]["X2"]}
        for k in ("p_WqP", "p_Wkv", "p_Wk", "p_Wv", "p_Wo_nz", "p_qgA", "p_kg"):
            m[k] = prep[k]
        for k in ("c_ident_b", "c_mask896", "c_bd4_b", "c_bd4T_b", "c_bdrep_b", "c_ones_row_f"):
            m[k] = consts[k]
        mapsC.append(m)
    resC = run_bass_kernel_spmd(ncC, mapsC, core_ids=list(range(B)))

    ncD = build_phaseD()
    mapsD = []
    for b in range(B):
        m = {"X3": resC.results[b]["X3"]}
        for k in ("p_gate_wT", "p_ebias_row", "p_w1T_1", "p_w3T_1", "p_w2T_1",
                  "p_sw1T", "p_sw3T", "p_sw2T"):
            m[k] = prep[k]
        for k in ("c_ident_b", "c_ident_f", "c_sel16_b", "c_ones_row_f",
                  "c_ones_col_f", "c_ones_col_b"):
            m[k] = consts[k]
        mapsD.append(m)
    resD = run_bass_kernel_spmd(ncD, mapsD, core_ids=list(range(B)))

    out = np.stack([resD.results[b]["OUT"] for b in range(B)], 0)
    return out
